# version 06
# baseline (speedup 1.0000x reference)
import math
import numpy as np
import ml_dtypes
import concourse.bass as bass
import concourse.mybir as mybir
from concourse.bass_utils import run_bass_kernel_spmd
from concourse.bass_core_rust import IndirectOffsetOnAxis

F32 = mybir.dt.float32
BF16 = mybir.dt.bfloat16
I32 = mybir.dt.int32
AF = mybir.ActivationFunctionType
ALU = mybir.AluOpType
AX = mybir.AxisListType
bf = ml_dtypes.bfloat16

NCORES = 8
T = 4096
D = 1024
H = 16
DH = 64
FF = 4096
NE = 16
CAP = 2560
EPP = 2
TS = 512
NTS = T // TS

ENGINE_ORDER = ["sync", "gpsimd", "vector", "scalar", "tensor"]


class Prog:

    NLANES = 8

    def __init__(self, nc):
        self.nc = nc
        self.ops = []
        self.lastw = {}
        self.readers = {}
        self.last_on_engine = {}
        self.dma_count = {}
        self._opkeys_r = []
        self._opkeys_w = []

    def op(self, engine, emit, reads=(), writes=(), dma=False, tag="", extra_deps=()):
        i = len(self.ops)
        deps = set(extra_deps)
        for k in reads:
            if k in self.lastw:
                deps.add(self.lastw[k])
        for k in writes:
            if k in self.lastw:
                deps.add(self.lastw[k])
            for r in self.readers.get(k, ()):
                deps.add(r)
        deps.discard(i)
        lane = None
        if dma:
            idx = self.dma_count.get(engine, 0)
            self.dma_count[engine] = idx + 1
            lane = idx % self.NLANES
        self.ops.append(dict(engine=engine, emit=emit, deps=deps,
                             inc=16 if dma else 1, tag=tag, lane=lane))
        self._opkeys_r.append(tuple(reads))
        self._opkeys_w.append(tuple(writes))
        for k in reads:
            self.readers.setdefault(k, []).append(i)
        for k in writes:
            self.lastw[k] = i
            self.readers[k] = []
        self.last_on_engine[engine] = i
        return i

    def interleave(self, s1, s2):
        end = len(self.ops)
        A = list(range(s1, s2))
        B = list(range(s2, end))
        if not A or not B:
            return
        aw = set()
        ar = set()
        for i in A:
            aw.update(k for k in self._opkeys_w[i])
            ar.update(k for k in self._opkeys_r[i])
        for i in B:
            bw = set(self._opkeys_w[i])
            br = set(self._opkeys_r[i])
            assert not (aw & bw) and not (aw & br) and not (ar & bw), (
                f"interleave ranges share keys: {sorted((aw & bw) | (aw & br) | (ar & bw))[:6]}")
        merged = []
        ia = ib = 0
        ratio = len(A) / (len(A) + len(B))
        na = nb = 0
        while ia < len(A) or ib < len(B):
            take_a = ib >= len(B) or (ia < len(A) and (na + nb == 0 or na / (na + nb) <= ratio))
            if take_a:
                merged.append(A[ia]); ia += 1; na += 1
            else:
                merged.append(B[ib]); ib += 1; nb += 1
        perm = list(range(s1)) + merged
        inv = {old: new for new, old in enumerate(perm)}
        self.ops = [self.ops[old] for old in perm]
        self._opkeys_r = [self._opkeys_r[old] for old in perm]
        self._opkeys_w = [self._opkeys_w[old] for old in perm]
        for o in self.ops:
            o["deps"] = {inv[d] for d in o["deps"]}
        self.lastw = {k: inv[v] for k, v in self.lastw.items()}
        self.readers = {k: [inv[v] for v in vs] for k, vs in self.readers.items()}
        self.last_on_engine = {}
        self.dma_count = {}
        for i, o in enumerate(self.ops):
            self.last_on_engine[o["engine"]] = i
            if o["lane"] is not None:
                idx = self.dma_count.get(o["engine"], 0)
                self.dma_count[o["engine"]] = idx + 1
                o["lane"] = idx % self.NLANES

    def hard_fence(self):
        snap = dict(self.last_on_engine)
        for e in ENGINE_ORDER:
            self.op(e, lambda en: en.nop(), extra_deps=set(snap.values()),
                    tag="fence")

    def range_fence(self, start):
        deps = {}
        for i in range(start, len(self.ops)):
            o = self.ops[i]
            k = (o["engine"], o["lane"])
            deps[k] = i
        dep_set = set(deps.values())
        if not dep_set:
            return
        for e in ENGINE_ORDER:
            self.op(e, lambda en: en.nop(), extra_deps=dep_set, tag="rfence")

    def finalize(self):
        nc = self.nc
        n = len(self.ops)
        ops = self.ops
        need = [False] * n
        for i, o in enumerate(ops):
            if o["inc"] == 16:
                need[i] = True
            for d in o["deps"]:
                if (ops[d]["engine"] == "tensor" and o["engine"] == "tensor"
                        and ops[d]["inc"] == 1):
                    continue
                need[d] = True
        engines = sorted({o["engine"] for o in ops},
                         key=ENGINE_ORDER.index)
        esem = {e: nc.alloc_semaphore(f"S_{e}") for e in engines}
        dsem = {}
        for e in sorted(self.dma_count):
            for l in range(self.NLANES):
                dsem[(e, l)] = nc.alloc_semaphore(f"D_{e}_{l}")
        token = [None] * n
        ecnt = {}
        lanecnt = {}
        lane_prev = [0] * n
        for i, o in enumerate(ops):
            e = o["engine"]
            if o["lane"] is not None:
                lk = (e, o["lane"])
                c = lanecnt.get(lk, 0)
                lane_prev[i] = c
                c += 16
                lanecnt[lk] = c
                token[i] = (("d", e, o["lane"]), c)
            else:
                c = ecnt.get(e, 0)
                if need[i]:
                    c += 1
                ecnt[e] = c
                token[i] = (("e", e), c)

        def semof(sk):
            return dsem[(sk[1], sk[2])] if sk[0] == "d" else esem[sk[1]]

        with nc.Block() as block:
            for e in engines:
                ops_e = [(i, o) for i, o in enumerate(ops) if o["engine"] == e]

                def body(eng, e=e, ops_e=ops_e):
                    waited = {}
                    for i, o in ops_e:
                        wv = {}
                        for d in o["deps"]:
                            de = ops[d]["engine"]
                            if de == "tensor" and e == "tensor" and ops[d]["inc"] == 1:
                                continue
                            sk, v = token[d]
                            if sk[0] == "e" and not need[d]:
                                sk, v = None, 0
                            if sk is not None and v > wv.get(sk, -1):
                                wv[sk] = v
                        if o["lane"] is not None and lane_prev[i] > 0:
                            sk = ("d", e, o["lane"])
                            if lane_prev[i] > wv.get(sk, -1):
                                wv[sk] = lane_prev[i]
                        for sk in sorted(wv):
                            v = wv[sk]
                            if v > waited.get(sk, 0):
                                eng.wait_ge(semof(sk), v)
                                waited[sk] = v
                        ins = o["emit"](eng)
                        if o["lane"] is not None:
                            ins.then_inc(semof(("d", e, o["lane"])), 16)
                        elif need[i]:
                            ins.then_inc(esem[e], 1)

                getattr(block, e)(body)


class Ctx:
    def __init__(self, nc):
        self.nc = nc
        self.p = Prog(nc)
        self.scopes = []

    def push_scope(self):
        self.scopes.append(([], len(self.p.ops)))

    def pop_scope(self):
        guards, start = self.scopes.pop()
        for g in reversed(guards):
            g.__exit__(None, None, None)
        self.p.range_fence(start)

    def salloc(self, name, shape, dtype, **kw):
        g = self.nc.sbuf_tensor(name, shape, dtype, **kw)
        t = g.__enter__()
        if self.scopes:
            self.scopes[-1][0].append(g)
        return t


def make_consts(nc):
    c = {}
    c["ident_bf"] = nc.inline_tensor(np.eye(128, dtype=bf), name="c_ident_bf")
    c["ident_f32"] = nc.inline_tensor(np.eye(128, dtype=np.float32), name="c_ident_f32")
    iu = np.triu(np.ones((128, 128), dtype=np.float32), k=0).astype(bf)
    c["triu_incl"] = nc.inline_tensor(iu, name="c_triu")
    for j in range(4):
        m = np.zeros((128, 512), dtype=np.float32)
        for k in range(128):
            m[k, j * 128 + k:] = 1.0
        c[f"tri01_{j}"] = nc.inline_tensor(m.astype(bf), name=f"c_tri01_{j}")
    c["iota16"] = nc.inline_tensor(
        np.tile(np.arange(16, dtype=np.float32), (128, 1)), name="c_iota16")
    c["ones_col_f32"] = nc.inline_tensor(np.ones((128, 1), dtype=np.float32),
                                         name="c_ones_col_f32")
    c["ones_1x128_f32"] = nc.inline_tensor(np.ones((1, 128), dtype=np.float32),
                                           name="c_ones_1x128_f32")
    su = np.triu(np.ones((8, 8), dtype=np.float32), k=1)
    c["striu8"] = nc.inline_tensor(su, name="c_striu8")
    c["ones_col_bf"] = nc.inline_tensor(np.ones((128, 1), dtype=bf), name="c_ones_col_bf")
    return c


def load_consts(ctx, names):
    nc, p = ctx.nc, ctx.p
    ctx.c = {}
    for nm in names:
        h = ctx.c_dram[nm]
        t = ctx.salloc(f"sb_{nm}", list(h.shape), h.dtype)
        ctx.c[nm] = t
        p.op("sync", lambda e, t=t, h=h: e.dma_start(out=t[:, :], in_=h.ap()),
             writes=[("c", nm)], dma=True, tag=f"ld_{nm}")


def emit_x_to_fm(ctx, x_in, xfm, key):
    nc, p = ctx.nc, ctx.p
    PS = ctx.PS
    ctx.push_scope()
    xt = ctx.salloc(f"x2f_in_{key}", [128, 1024], F32)
    ot = ctx.salloc(f"x2f_out_{key}", [128, 128], F32)
    for tt in range(T // 128):
        p.op("sync", lambda e, tt=tt: e.dma_start(
            out=xt[:, :], in_=x_in[tt * 128:(tt + 1) * 128, :]),
            writes=[("x2f_in", key)], dma=True)
        for cdm in range(8):
            bank = cdm % 2 + 4
            p.op("tensor", lambda e, cdm=cdm, bank=bank: e.transpose(
                PS[0:128, bank * 512:bank * 512 + 128], xt[:, cdm * 128:(cdm + 1) * 128],
                ctx.c["ident_f32"][:, :]),
                reads=[("x2f_in", key), ("c", "ident_f32")], writes=[("ps", bank)])
            p.op("vector", lambda e, bank=bank: e.tensor_copy(
                ot[:, :], PS[0:128, bank * 512:bank * 512 + 128]),
                reads=[("ps", bank)], writes=[("x2f_out", key)])
            p.op("sync", lambda e, tt=tt, cdm=cdm: e.dma_start(
                out=xfm[cdm * 128:(cdm + 1) * 128, tt * 128:(tt + 1) * 128], in_=ot[:, :]),
                reads=[("x2f_out", key)], writes=[("xfm", key, cdm, tt % 8)], dma=True)
    ctx.pop_scope()


def emit_fm_to_tok_f32(ctx, src_fm, dst_tok, key):
    nc, p = ctx.nc, ctx.p
    PS = ctx.PS
    ctx.push_scope()
    it = ctx.salloc(f"f2t_in_{key}", [128, T], F32)
    ot = ctx.salloc(f"f2t_out_{key}", [128, 128], F32)
    for cdm in range(8):
        p.op("sync", lambda e, cdm=cdm: e.dma_start(
            out=it[:, :], in_=src_fm[cdm * 128:(cdm + 1) * 128, :]),
            reads=[("f2t-src", key, cdm)], writes=[("f2t_in", key)], dma=True)
        for tt in range(T // 128):
            bank = tt % 2 + 4
            p.op("tensor", lambda e, tt=tt, bank=bank: e.transpose(
                PS[0:128, bank * 512:bank * 512 + 128], it[:, tt * 128:(tt + 1) * 128],
                ctx.c["ident_f32"][:, :]),
                reads=[("f2t_in", key), ("c", "ident_f32")], writes=[("ps", bank)])
            p.op("vector", lambda e, bank=bank: e.tensor_copy(
                ot[:, :], PS[0:128, bank * 512:bank * 512 + 128]),
                reads=[("ps", bank)], writes=[("f2t_out", key)])
            p.op("sync", lambda e, tt=tt, cdm=cdm: e.dma_start(
                out=dst_tok[tt * 128:(tt + 1) * 128, cdm * 128:(cdm + 1) * 128], in_=ot[:, :]),
                reads=[("f2t_out", key)], writes=[("f2t-out", key, tt % 8, cdm)], dma=True)
    ctx.pop_scope()


def emit_ln(ctx, xfm, xnfm, key, gate_cb=None):
    nc, p = ctx.nc, ctx.p
    PS = ctx.PS
    ctx.push_scope()
    xs = [ctx.salloc(f"ln_x_{key}_{c}", [128, TS], F32) for c in range(8)]
    xsq = ctx.salloc(f"ln_xsq_{key}", [128, TS], F32)
    rowbuf = ctx.salloc(f"ln_rows_{key}", [1, 6 * TS], F32)
    mu = rowbuf[0:1, 0:TS]
    msq = rowbuf[0:1, TS:2 * TS]
    mu2 = rowbuf[0:1, 2 * TS:3 * TS]
    var = rowbuf[0:1, 3 * TS:4 * TS]
    sd = rowbuf[0:1, 4 * TS:5 * TS]
    rstd = rowbuf[0:1, 5 * TS:6 * TS]
    Mb = ctx.salloc(f"ln_Mb_{key}", [128, TS], F32)
    Rb = ctx.salloc(f"ln_Rb_{key}", [128, TS], F32)
    tmp = ctx.salloc(f"ln_tmp_{key}", [128, TS], F32)
    xnb = ctx.salloc(f"ln_xnb_{key}", [128, TS], BF16)
    eps = ctx.salloc(f"ln_eps_{key}", [1, 1], F32)
    p.op("vector", lambda e: e.memset(eps[:, :], 1e-5), writes=[("lneps", key)])
    oc = ctx.c["ones_col_f32"]
    o1x = ctx.c["ones_1x128_f32"]

    for ts in range(NTS):
        for c in range(8):
            p.op("sync", lambda e, c=c, ts=ts: e.dma_start(
                out=xs[c][:, :], in_=xfm[c * 128:(c + 1) * 128, ts * TS:(ts + 1) * TS]),
                reads=[("xfm-src", key, c, ts)], writes=[("lnx", key, c)], dma=True)
        for c in range(8):
            p.op("tensor", lambda e, c=c: e.matmul(
                PS[0:1, 0:TS], oc[:, :], xs[c][:, :], start=(c == 0), stop=(c == 7)),
                reads=[("lnx", key, c), ("c", "ones_col_f32")], writes=[("ps", 0)])
            p.op("scalar", lambda e, c=c: e.activation(xsq[:, :], xs[c][:, :], AF.Square),
                 reads=[("lnx", key, c)], writes=[("lnxsq", key)])
            p.op("tensor", lambda e, c=c: e.matmul(
                PS[0:1, 512:512 + TS], oc[:, :], xsq[:, :], start=(c == 0), stop=(c == 7)),
                reads=[("lnxsq", key), ("c", "ones_col_f32")], writes=[("ps", 1)])
        p.op("vector", lambda e: e.tensor_scalar_mul(mu, PS[0:1, 0:TS], 1.0 / D),
             reads=[("ps", 0)], writes=[("lnrow", key, "mu")])
        p.op("vector", lambda e: e.tensor_scalar_mul(msq, PS[0:1, 512:512 + TS], 1.0 / D),
             reads=[("ps", 1)], writes=[("lnrow", key, "msq")])
        p.op("scalar", lambda e: e.activation(mu2, mu, AF.Square),
             reads=[("lnrow", key, "mu")], writes=[("lnrow", key, "mu2")])
        p.op("vector", lambda e: e.tensor_sub(var, msq, mu2),
             reads=[("lnrow", key, "msq"), ("lnrow", key, "mu2")],
             writes=[("lnrow", key, "var")])
        p.op("scalar", lambda e: e.activation(sd, var, AF.Sqrt, bias=eps[0:1, 0:1]),
             reads=[("lnrow", key, "var"), ("lneps", key)], writes=[("lnrow", key, "sd")])
        p.op("vector", lambda e: e.reciprocal(rstd, sd),
             reads=[("lnrow", key, "sd")], writes=[("lnrow", key, "rstd")])
        p.op("tensor", lambda e: e.matmul(PS[0:128, 1024:1024 + TS], o1x[:, :], mu,
                                          start=True, stop=True),
             reads=[("lnrow", key, "mu"), ("c", "ones_1x128_f32")], writes=[("ps", 2)])
        p.op("vector", lambda e: e.tensor_copy(Mb[:, :], PS[0:128, 1024:1024 + TS]),
             reads=[("ps", 2)], writes=[("lnMb", key)])
        p.op("tensor", lambda e: e.matmul(PS[0:128, 1536:1536 + TS], o1x[:, :], rstd,
                                          start=True, stop=True),
             reads=[("lnrow", key, "rstd"), ("c", "ones_1x128_f32")], writes=[("ps", 3)])
        p.op("scalar", lambda e: e.copy(Rb[:, :], PS[0:128, 1536:1536 + TS]),
             reads=[("ps", 3)], writes=[("lnRb", key)])
        for c in range(8):
            p.op("vector", lambda e, c=c: e.tensor_sub(tmp[:, :], xs[c][:, :], Mb[:, :]),
                 reads=[("lnx", key, c), ("lnMb", key)], writes=[("lntmp", key)])
            p.op("vector", lambda e: e.tensor_mul(xnb[:, :], tmp[:, :], Rb[:, :]),
                 reads=[("lntmp", key), ("lnRb", key)], writes=[("lnxnb", key)])
            p.op("sync", lambda e, c=c, ts=ts: e.dma_start(
                out=xnfm[c * 128:(c + 1) * 128, ts * TS:(ts + 1) * TS], in_=xnb[:, :]),
                reads=[("lnxnb", key)], writes=[("xnfm", key, c, ts)], dma=True)
        if gate_cb is not None:
            gate_cb(ts, mu, rstd, Rb, xs)
    ctx.pop_scope()


def emit_qkv(ctx, xnfm, qfm, kfm, wq_src, wk_src, wv_src, vptiles, key):
    nc, p = ctx.nc, ctx.p
    PS = ctx.PS
    ctx.push_scope()
    wqt = [ctx.salloc(f"qk_wq_{key}_{c}", [128, 1024], BF16, is_transpose_dest=True) for c in range(8)]
    wkt = [ctx.salloc(f"qk_wk_{key}_{c}", [128, 1024], BF16, is_transpose_dest=True) for c in range(8)]
    wvt = [ctx.salloc(f"qk_wv_{key}_{c}", [128, 1024], BF16, is_transpose_dest=True) for c in range(8)]
    for c in range(8):
        wq_src(c, wqt[c], ("qkvw", key, "q", c))
        wk_src(c, wkt[c], ("qkvw", key, "k", c))
        wv_src(c, wvt[c], ("qkvw", key, "v", c))
    xns = [ctx.salloc(f"qk_xn_{key}_{c}", [128, TS], BF16) for c in range(8)]
    ev = ctx.salloc(f"qk_ev_{key}", [128, 2 * TS], BF16)
    for tt in range(T // 128):
        p.op("vector", lambda e, tt=tt: e.memset(vptiles[tt][:, :], 1.0),
             writes=[("vp", tt)])
    for ts in range(NTS):
        for c in range(8):
            p.op("sync", lambda e, c=c, ts=ts: e.dma_start(
                out=xns[c][:, :], in_=xnfm[c * 128:(c + 1) * 128, ts * TS:(ts + 1) * TS]),
                reads=[("xnfm-src", key, c, ts)], writes=[("qkvxn", key, c)], dma=True)
        for wt, dstfm, nm in [(wqt, qfm, "q"), (wkt, kfm, "k")]:
            for hp in range(8):
                bank = hp % 2
                for c in range(8):
                    p.op("tensor", lambda e, wt=wt, hp=hp, c=c, bank=bank: e.matmul(
                        PS[0:128, bank * 512:bank * 512 + TS],
                        wt[c][:, hp * 128:(hp + 1) * 128], xns[c][:, :],
                        start=(c == 0), stop=(c == 7)),
                        reads=[("qkvw", key, nm, c), ("qkvxn", key, c)],
                        writes=[("ps", bank)])
                half = hp % 2
                p.op("vector", lambda e, half=half, bank=bank: e.tensor_copy(
                    ev[:, half * TS:(half + 1) * TS], PS[0:128, bank * 512:bank * 512 + TS]),
                    reads=[("ps", bank)], writes=[("qkvev", key, half)])
                p.op("sync", lambda e, hp=hp, ts=ts, dstfm=dstfm, half=half, nm=nm: e.dma_start(
                    out=dstfm[hp * 128:(hp + 1) * 128, ts * TS:(ts + 1) * TS],
                    in_=ev[:, half * TS:(half + 1) * TS]),
                    reads=[("qkvev", key, half)], writes=[(nm + "fm", key, hp, ts)], dma=True)
        for sub in range(TS // 128):
            tt = ts * (TS // 128) + sub
            bank = 2 + (sub % 2) * 2
            for half in range(2):
                for c in range(8):
                    p.op("tensor", lambda e, c=c, bank=bank, sub=sub, half=half: e.matmul(
                        PS[0:128, (bank + half) * 512:(bank + half) * 512 + 512],
                        xns[c][:, sub * 128:(sub + 1) * 128],
                        wvt[c][:, half * 512:(half + 1) * 512],
                        start=(c == 0), stop=(c == 7)),
                        reads=[("qkvxn", key, c), ("qkvw", key, "v", c)],
                        writes=[("ps", bank + half)])
            p.op("vector", lambda e, tt=tt, bank=bank: e.tensor_copy(
                vptiles[tt][:, :].rearrange("p (h v) -> p h v", h=H, v=65)[:, :, 0:64],
                PS[0:128, bank * 512:bank * 512 + 1024].rearrange("p (h v) -> p h v", h=H, v=64)),
                reads=[("ps", bank), ("ps", bank + 1)], writes=[("vp", tt)])
    ctx.pop_scope()


def emit_flash(ctx, qfm, kfm, vptiles, avfm, causal, key):
    nc, p = ctx.nc, ctx.p
    PS = ctx.PS
    ctx.push_scope()
    kh = [ctx.salloc(f"fl_k_{key}_{i}", [64, T], BF16) for i in range(2)]
    qh = [ctx.salloc(f"fl_q_{key}_{i}", [64, T], BF16) for i in range(2)]
    eb = [ctx.salloc(f"fl_e_{key}_{i}", [128, 2 * TS], BF16) for i in range(2)]
    em = [ctx.salloc(f"fl_em_{key}_{i}", [128, 2 * TS], BF16) for i in range(2)]
    recl = ctx.salloc(f"fl_recl_{key}", [1, TS], F32)
    recb = ctx.salloc(f"fl_recb_{key}", [1, TS], BF16)
    bcast = ctx.salloc(f"fl_bc_{key}", [64, TS], F32)
    avh = [ctx.salloc(f"fl_avh_{key}_{i}", [64, TS], BF16) for i in range(2)]
    ones64 = ctx.salloc(f"fl_o64_{key}", [1, 64], BF16)
    bias20 = ctx.salloc(f"fl_b20_{key}", [128, 1], F32)
    p.op("vector", lambda e: e.memset(ones64[:, :], 1.0), writes=[("flo64", key)])
    p.op("vector", lambda e: e.memset(bias20[:, :], -20.0), writes=[("flb20", key)])

    def emit_scores_pair(h, hb, qs, kt0, kt1, pb):
        for u, kt in enumerate((kt0, kt1)):
            bank = 2 * pb + u
            p.op("tensor", lambda e, kt=kt, qs=qs, bank=bank, hb=hb: e.matmul(
                PS[0:128, bank * 512:bank * 512 + TS],
                kh[hb][:, kt * 128:(kt + 1) * 128], qh[hb][:, qs * TS:(qs + 1) * TS],
                start=True, stop=True),
                reads=[("flk", key, hb), ("flq", key, hb)], writes=[("ps", bank)])
        p.op("scalar", lambda e, pb=pb: e.activation(
            eb[pb][:, :], PS[0:128, 2 * pb * 512:2 * pb * 512 + 2 * TS], AF.Exp,
            bias=bias20[:, 0:1], scale=0.125),
            reads=[("ps", 2 * pb), ("ps", 2 * pb + 1), ("flb20", key)],
            writes=[("fle", key, pb)])
        for u, kt in enumerate((kt0, kt1)):
            if causal and kt >= qs * 4:
                jj = kt - qs * 4
                p.op("vector", lambda e, pb=pb, u=u, jj=jj: e.tensor_mul(
                    em[pb][:, u * TS:(u + 1) * TS], eb[pb][:, u * TS:(u + 1) * TS],
                    ctx.c[f"tri01_{jj}"][:, :]),
                    reads=[("fle", key, pb), ("c", f"tri01_{jj}")],
                    writes=[("flem", key, pb, u)])

    def emit_av_pair(h, hb, qs, kt0, kt1, pb, j0, nkt, avbank):
        for u, kt in enumerate((kt0, kt1)):
            if causal and kt >= qs * 4:
                esrc = em[pb][:, u * TS:(u + 1) * TS]
                ekey = ("flem", key, pb, u)
            else:
                esrc = eb[pb][:, u * TS:(u + 1) * TS]
                ekey = ("fle", key, pb)
            j = j0 + u
            p.op("tensor", lambda e, kt=kt, esrc=esrc, avbank=avbank, h=h, j=j, nkt=nkt: e.matmul(
                PS[0:65, avbank * 512:avbank * 512 + TS],
                vptiles[kt][:, h * 65:(h + 1) * 65], esrc,
                start=(j == 0), stop=(j == nkt - 1)),
                reads=[("vp", kt), ekey], writes=[("ps", avbank)])

    def mk_norm(h, qs, avbank, ab):
        def reciprocal():
            p.op("vector", lambda e, avbank=avbank: e.reciprocal(
                recl[:, :], PS[64:65, avbank * 512:avbank * 512 + TS]),
                reads=[("ps", avbank)], writes=[("flrecl", key)])
            p.op("vector", lambda e: e.tensor_copy(recb[:, :], recl[:, :]),
                 reads=[("flrecl", key)], writes=[("flrecb", key)])

        def bcast_mm():
            p.op("tensor", lambda e: e.matmul(
                PS[0:64, 6 * 512:6 * 512 + TS], ones64[:, :], recb[:, :],
                start=True, stop=True),
                reads=[("flrecb", key), ("flo64", key)], writes=[("ps", 6)])

        def tail():
            p.op("scalar", lambda e: e.copy(bcast[:, :], PS[0:64, 6 * 512:6 * 512 + TS]),
                 reads=[("ps", 6)], writes=[("flbc", key)])
            p.op("vector", lambda e, avbank=avbank, ab=ab: e.tensor_mul(
                avh[ab][:, :], PS[0:64, avbank * 512:avbank * 512 + TS], bcast[:, :]),
                reads=[("ps", avbank), ("flbc", key)], writes=[("flavh", key, ab)])
            p.op("sync", lambda e, h=h, qs=qs, ab=ab: e.dma_start(
                out=avfm[h * 64:(h + 1) * 64, qs * TS:(qs + 1) * TS], in_=avh[ab][:, :]),
                reads=[("flavh", key, ab)], writes=[("avfm", key, h, qs)], dma=True)

        return (reciprocal, bcast_mm, tail)

    pending = None
    for h in range(H):
        hb = h % 2
        p.op("sync", lambda e, h=h, hb=hb: e.dma_start(
            out=kh[hb][:, :], in_=kfm[h * 64:(h + 1) * 64, :]),
            reads=[("kfm-all", key)], writes=[("flk", key, hb)], dma=True)
        p.op("sync", lambda e, h=h, hb=hb: e.dma_start(
            out=qh[hb][:, :], in_=qfm[h * 64:(h + 1) * 64, :]),
            reads=[("qfm-all", key)], writes=[("flq", key, hb)], dma=True)
        for qs in range(NTS):
            avbank = 4 + qs % 2
            ab = qs % 2
            kts = list(range(0, (qs * 4 + 4) if causal else 32))
            nkt = len(kts)
            pairs = [(kts[i], kts[i + 1]) for i in range(0, nkt, 2)]
            npair = len(pairs)
            emit_scores_pair(h, hb, qs, *pairs[0], 0)
            if pending is not None:
                pending[0]()
            for pi in range(1, npair):
                emit_scores_pair(h, hb, qs, *pairs[pi], pi % 2)
                if pending is not None and pi == 1:
                    pending[1]()
                    pending[2]()
                    pending = None
                emit_av_pair(h, hb, qs, *pairs[pi - 1], (pi - 1) % 2,
                             2 * (pi - 1), nkt, avbank)
            emit_av_pair(h, hb, qs, *pairs[npair - 1], (npair - 1) % 2,
                         2 * (npair - 1), nkt, avbank)
            if pending is not None:
                pending[1]()
                pending[2]()
            pending = mk_norm(h, qs, avbank, ab)
    pending[0]()
    pending[1]()
    pending[2]()
    ctx.pop_scope()


def emit_quantize(ctx, w_in, rows, cols, wq_dram, key, wqT_dram=None, scoped=True):
    nc, p = ctx.nc, ctx.p
    PS = ctx.PS
    if scoped:
        ctx.push_scope()
    CB = 1024
    ncb = cols // CB
    nt = rows // 128
    wt = ctx.salloc(f"qz_w_{key}", [128, CB], F32)
    at = ctx.salloc(f"qz_a_{key}", [128, CB], F32)
    acc = ctx.salloc(f"qz_acc_{key}", [128, nt * ncb], F32)
    accs = ctx.salloc(f"qz_accs_{key}", [128, 1], F32)
    gam = ctx.salloc(f"qz_gam_{key}", [1, 1], F32)
    invg = ctx.salloc(f"qz_invg_{key}", [1, 1], F32)
    invgb = ctx.salloc(f"qz_invgb_{key}", [128, 1], F32)
    z = ctx.salloc(f"qz_z_{key}", [128, CB], F32)
    a01 = ctx.salloc(f"qz_a01_{key}", [128, CB], F32)
    b01 = ctx.salloc(f"qz_b01_{key}", [128, CB], F32)
    q = ctx.salloc(f"qz_q_{key}", [128, CB], BF16)
    if wqT_dram is not None:
        qf = ctx.salloc(f"qz_qf_{key}", [128, CB], F32)
        qtb = ctx.salloc(f"qz_qtb_{key}", [128, 128], BF16)
    oc = ctx.c["ones_col_f32"]
    o1x = ctx.c["ones_1x128_f32"]
    k = 0
    for t in range(nt):
        for cb in range(ncb):
            p.op("sync", lambda e, t=t, cb=cb: e.dma_start(
                out=wt[:, :], in_=w_in[t * 128:(t + 1) * 128, cb * CB:(cb + 1) * CB]),
                writes=[("qzw", key)], dma=True)
            p.op("scalar", lambda e: e.activation(at[:, :], wt[:, :], AF.Abs),
                 reads=[("qzw", key)], writes=[("qza", key)])
            p.op("vector", lambda e, k=k: e.reduce_sum(acc[:, k:k + 1], at[:, :], axis=AX.X),
                 reads=[("qza", key)], writes=[("qzacc", key, k)])
            k += 1
    p.op("vector", lambda e: e.reduce_sum(accs[:, :], acc[:, :], axis=AX.X),
         reads=[("qzacc", key, k2) for k2 in range(nt * ncb)], writes=[("qzaccs", key)])
    p.op("tensor", lambda e: e.matmul(PS[0:1, 6 * 512:6 * 512 + 1], accs[:, :], oc[:, 0:1],
                                      start=True, stop=True),
         reads=[("qzaccs", key), ("c", "ones_col_f32")], writes=[("ps", 6)])
    p.op("vector", lambda e: e.tensor_scalar(gam[:, :], PS[0:1, 6 * 512:6 * 512 + 1],
                                             1.0 / (rows * cols), 1e-5, ALU.mult, ALU.add),
         reads=[("ps", 6)], writes=[("qzgam", key)])
    p.op("vector", lambda e: e.reciprocal(invg[:, :], gam[:, :]),
         reads=[("qzgam", key)], writes=[("qzinvg", key)])
    p.op("tensor", lambda e: e.matmul(PS[0:128, 7 * 512:7 * 512 + 1], o1x[:, :], invg[:, :],
                                      start=True, stop=True),
         reads=[("qzinvg", key), ("c", "ones_1x128_f32")], writes=[("ps", 7)])
    p.op("vector", lambda e: e.tensor_copy(invgb[:, :], PS[0:128, 7 * 512:7 * 512 + 1]),
         reads=[("ps", 7)], writes=[("qzinvgb", key)])
    for t in range(nt):
        for cb in range(ncb):
            p.op("sync", lambda e, t=t, cb=cb: e.dma_start(
                out=wt[:, :], in_=w_in[t * 128:(t + 1) * 128, cb * CB:(cb + 1) * CB]),
                writes=[("qzw", key)], dma=True)
            p.op("scalar", lambda e: e.activation(z[:, :], wt[:, :], AF.Copy,
                                                  scale=invgb[:, 0:1]),
                 reads=[("qzw", key), ("qzinvgb", key)], writes=[("qzz", key)])
            p.op("vector", lambda e: e.tensor_scalar(a01[:, :], z[:, :], 0.5, None, ALU.is_gt),
                 reads=[("qzz", key)], writes=[("qza01", key)])
            p.op("vector", lambda e: e.tensor_scalar(b01[:, :], z[:, :], -0.5, None, ALU.is_lt),
                 reads=[("qzz", key)], writes=[("qzb01", key)])
            p.op("vector", lambda e: e.tensor_sub(q[:, :], a01[:, :], b01[:, :]),
                 reads=[("qza01", key), ("qzb01", key)], writes=[("qzq", key)])
            p.op("sync", lambda e, t=t, cb=cb: e.dma_start(
                out=wq_dram[t * 128:(t + 1) * 128, cb * CB:(cb + 1) * CB], in_=q[:, :]),
                reads=[("qzq", key)], writes=[("qdram", key, t)], dma=True)
            if wqT_dram is not None:
                p.op("vector", lambda e: e.tensor_sub(qf[:, :], a01[:, :], b01[:, :]),
                     reads=[("qza01", key), ("qzb01", key)], writes=[("qzqf", key)])
                for b in range(CB // 128):
                    p.op("tensor", lambda e, b=b: e.transpose(
                        PS[0:128, 6 * 512:6 * 512 + 128], qf[:, b * 128:(b + 1) * 128],
                        ctx.c["ident_f32"][:, :]),
                        reads=[("qzqf", key), ("c", "ident_f32")], writes=[("ps", 6)])
                    p.op("vector", lambda e, b=b: e.tensor_copy(
                        qtb[:, :], PS[0:128, 6 * 512:6 * 512 + 128]),
                        reads=[("ps", 6)], writes=[("qzqtb", key)])
                    p.op("sync", lambda e, t=t, cb=cb, b=b: e.dma_start(
                        out=wqT_dram[cb * CB + b * 128:cb * CB + (b + 1) * 128,
                                     t * 128:(t + 1) * 128], in_=qtb[:, :]),
                        reads=[("qzqtb", key)], writes=[("qtdram", key, t, cb, b)], dma=True)
    if scoped:
        ctx.pop_scope()


def emit_epilogue_l0(ctx, xfm, xnfm, avfm, wob, w1qb, w2qtb, b1_in, b2_in, x1fm, key):
    nc, p = ctx.nc, ctx.p
    PS = ctx.PS
    ctx.push_scope()
    wot = [ctx.salloc(f"ep_wot_{key}_{h}", [64, 1024], BF16) for h in range(H)]
    wtmp = ctx.salloc(f"ep_wtmp_{key}", [128, 64], F32)
    for h in range(H):
        for c in range(8):
            p.op("sync", lambda e, h=h, c=c: e.dma_start(
                out=wtmp[:, :], in_=wob[h * 1024 + c * 128: h * 1024 + (c + 1) * 128, :]),
                writes=[("epwtmp", key)], dma=True)
            p.op("tensor", lambda e: e.transpose(PS[0:64, 7 * 512:7 * 512 + 128], wtmp[:, :],
                                                 ctx.c["ident_f32"][:, :]),
                 reads=[("epwtmp", key), ("c", "ident_f32")], writes=[("ps", 7)])
            p.op("vector", lambda e, h=h, c=c: e.tensor_copy(
                wot[h][:, c * 128:(c + 1) * 128], PS[0:64, 7 * 512:7 * 512 + 128]),
                reads=[("ps", 7)], writes=[("epwot", key, h)])
    b1s = ctx.salloc(f"ep_b1s_{key}", [32, 128], F32)
    b1t = ctx.salloc(f"ep_b1t_{key}", [128, 32], F32)
    p.op("sync", lambda e: e.dma_start(out=b1s[:, :], in_=b1_in),
         writes=[("epb1s", key)], dma=True)
    p.op("tensor", lambda e: e.transpose(PS[0:128, 5 * 512:5 * 512 + 32], b1s[:, :],
                                         ctx.c["ident_f32"][0:32, 0:32]),
         reads=[("epb1s", key), ("c", "ident_f32")], writes=[("ps", 5)])
    p.op("vector", lambda e: e.tensor_copy(b1t[:, :], PS[0:128, 5 * 512:5 * 512 + 32]),
         reads=[("ps", 5)], writes=[("epb1t", key)])
    b2row = ctx.salloc(f"ep_b2_{key}", [1, 1024], BF16)
    p.op("gpsimd", lambda e: e.dma_start(out=b2row[:, :], in_=b2_in),
         writes=[("epb2", key)], dma=True)
    onesr = ctx.salloc(f"ep_ones_{key}", [1, TS], BF16)
    p.op("vector", lambda e: e.memset(onesr[:, :], 1.0), writes=[("epones", key)])
    w1qt = [ctx.salloc(f"ep_w1qt_{key}_{c}", [128, FF], BF16, is_transpose_dest=True)
            for c in range(8)]
    for c in range(8):
        p.op("sync", lambda e, c=c: e.dma_start_transpose(
            w1qt[c][:, :], w1qb[:, c * 128:(c + 1) * 128]),
            reads=[("qdram", ("l0w1",), t) for t in range(32)],
            writes=[("epw1qt", key, c)], dma=True)
    w2qt = [ctx.salloc(f"ep_w2qt_{key}_{i}", [128, 1024], BF16) for i in range(8)]
    xns = [ctx.salloc(f"ep_xn_{key}_{c}", [128, TS], BF16) for c in range(8)]
    hsl = [ctx.salloc(f"ep_h_{key}_{f}", [128, TS], BF16) for f in range(32)]
    avt = [ctx.salloc(f"ep_av_{key}_{i}", [64, TS], BF16) for i in range(4)]
    xres = [ctx.salloc(f"ep_xres_{key}_{i}", [128, TS], F32) for i in range(2)]
    osum = [ctx.salloc(f"ep_osum_{key}_{i}", [128, TS], F32) for i in range(2)]

    for ts in range(NTS):
        for c in range(8):
            p.op("sync", lambda e, c=c, ts=ts: e.dma_start(
                out=xns[c][:, :], in_=xnfm[c * 128:(c + 1) * 128, ts * TS:(ts + 1) * TS]),
                reads=[("xnfm-src2", key, c, ts)], writes=[("epxn", key, c)], dma=True)
        for fc in range(32):
            bank = 4 + fc % 2
            for c in range(8):
                p.op("tensor", lambda e, fc=fc, c=c, bank=bank: e.matmul(
                    PS[0:128, bank * 512:(bank) * 512 + TS],
                    w1qt[c][:, fc * 128:(fc + 1) * 128], xns[c][:, :],
                    start=(c == 0), stop=(c == 7)),
                    reads=[("epw1qt", key, c), ("epxn", key, c)], writes=[("ps", bank)])
            p.op("scalar", lambda e, fc=fc, bank=bank: e.activation(
                hsl[fc][:, :], PS[0:128, bank * 512:bank * 512 + TS], AF.Relu,
                bias=b1t[:, fc:fc + 1]),
                reads=[("ps", bank), ("epb1t", key)], writes=[("eph", key, fc)])
        for h in range(H):
            hb = h % 4
            p.op("sync", lambda e, h=h, ts=ts, hb=hb: e.dma_start(
                out=avt[hb][:, :], in_=avfm[h * 64:(h + 1) * 64, ts * TS:(ts + 1) * TS]),
                reads=[("avfm", key, h, ts)], writes=[("epav", key, hb)], dma=True)
            for ds in range(8):
                p.op("tensor", lambda e, h=h, ds=ds, hb=hb: e.matmul(
                    PS[0:128, ds * 512:ds * 512 + TS],
                    wot[h][:, ds * 128:(ds + 1) * 128], avt[hb][:, :],
                    start=(h == 0), stop=False),
                    reads=[("epwot", key, h), ("epav", key, hb)], writes=[("ps", ds)])
        for fc in range(32):
            wi = fc % 8
            p.op("sync", lambda e, fc=fc, wi=wi: e.dma_start(
                out=w2qt[wi][:, :], in_=w2qtb[fc * 128:(fc + 1) * 128, :]),
                reads=[("qtdram", ("l0w2",), t2, cb2, b2_) for t2 in range(8)
                       for cb2 in range(4) for b2_ in range(8)],
                writes=[("epw2buf", key, wi)], dma=True)
            for ds in range(8):
                p.op("tensor", lambda e, fc=fc, ds=ds, wi=wi: e.matmul(
                    PS[0:128, ds * 512:ds * 512 + TS],
                    w2qt[wi][:, ds * 128:(ds + 1) * 128], hsl[fc][:, :],
                    start=False, stop=False),
                    reads=[("epw2buf", key, wi), ("eph", key, fc)], writes=[("ps", ds)])
        for ds in range(8):
            db = ds % 2
            p.op("tensor", lambda e, ds=ds: e.matmul(
                PS[0:128, ds * 512:ds * 512 + TS],
                b2row[0:1, ds * 128:(ds + 1) * 128], onesr[:, :],
                start=False, stop=True),
                reads=[("epb2", key), ("epones", key)], writes=[("ps", ds)])
            p.op("sync", lambda e, ds=ds, ts=ts, db=db: e.dma_start(
                out=xres[db][:, :], in_=xfm[ds * 128:(ds + 1) * 128, ts * TS:(ts + 1) * TS]),
                reads=[("xfm-src3", key, ds, ts)], writes=[("epxres", key, db)], dma=True)
            p.op("vector", lambda e, ds=ds, db=db: e.tensor_add(
                osum[db][:, :], PS[0:128, ds * 512:ds * 512 + TS], xres[db][:, :]),
                reads=[("ps", ds), ("epxres", key, db)], writes=[("eposum", key, db)])
            p.op("sync", lambda e, ds=ds, ts=ts, db=db: e.dma_start(
                out=x1fm[ds * 128:(ds + 1) * 128, ts * TS:(ts + 1) * TS], in_=osum[db][:, :]),
                reads=[("eposum", key, db)], writes=[("x1fm", key, ds, ts)], dma=True)
    ctx.pop_scope()


def emit_gate_router(ctx, gwT, negs, gate_bufs, key):
    nc, p = ctx.nc, ctx.p
    PS = ctx.PS
    (gsl, gtok, mx, nmx, oh, cand, eidx, ohb, eg, seg, prob, rnk, rsel,
     offs, probsb, eidxsb, lranksb, routet) = gate_bufs
    o1x = ctx.c["ones_1x128_f32"]
    ocbf = ctx.c["ones_col_bf"]
    iot = ctx.c["iota16"]
    triu = ctx.c["triu_incl"]
    p.op("vector", lambda e: e.memset(offs[:, :], 0.0), writes=[("goffs", key)])

    def gate_cb(ts, mu, rstd, Rb, xs):
        for c in range(8):
            p.op("tensor", lambda e, c=c: e.matmul(
                PS[0:16, 6 * 512:6 * 512 + TS], gwT[c][:, :], xs[c][:, :],
                start=(c == 0), stop=False),
                reads=[("gwT", key, c), ("lnx", key, c)], writes=[("ps", 6)])
        p.op("tensor", lambda e: e.matmul(
            PS[0:16, 6 * 512:6 * 512 + TS], negs[:, :], mu, start=False, stop=True),
            reads=[("gnegs", key), ("lnrow", key, "mu")], writes=[("ps", 6)])
        p.op("vector", lambda e: e.tensor_mul(gsl[:, :], PS[0:16, 6 * 512:6 * 512 + TS],
                                              Rb[0:16, :]),
             reads=[("ps", 6), ("lnRb", key)], writes=[("ggsl", key)])
        for sub in range(TS // 128):
            tt = ts * (TS // 128) + sub
            p.op("tensor", lambda e, sub=sub: e.transpose(
                PS[0:128, 5 * 512:5 * 512 + 16], gsl[:, sub * 128:(sub + 1) * 128],
                ctx.c["ident_f32"][0:16, 0:16]),
                reads=[("ggsl", key), ("c", "ident_f32")], writes=[("ps", 5)])
            p.op("vector", lambda e: e.tensor_copy(gtok[:, :], PS[0:128, 5 * 512:5 * 512 + 16]),
                 reads=[("ps", 5)], writes=[("ggtok", key)])
            p.op("vector", lambda e: e.reduce_max(mx[:, :], gtok[:, :], axis=AX.X),
                 reads=[("ggtok", key)], writes=[("gmx", key)])
            p.op("vector", lambda e: e.tensor_scalar(oh[:, :], gtok[:, :], mx[:, 0:1],
                                                     None, ALU.is_ge),
                 reads=[("ggtok", key), ("gmx", key)], writes=[("goh", key)])
            p.op("vector", lambda e: e.tensor_scalar(cand[:, :], oh[:, :], -1e9, 1e9,
                                                     ALU.mult, ALU.add),
                 reads=[("goh", key)], writes=[("gcand", key)])
            p.op("vector", lambda e: e.tensor_add(cand[:, :], cand[:, :], iot[:, :]),
                 reads=[("gcand", key), ("c", "iota16")], writes=[("gcand", key)])
            p.op("vector", lambda e: e.tensor_reduce(eidx[:, :], cand[:, :], axis=AX.X,
                                                     op=ALU.min),
                 reads=[("gcand", key)], writes=[("geidx", key)])
            p.op("vector", lambda e: e.tensor_scalar(oh[:, :], iot[:, :], eidx[:, 0:1],
                                                     None, ALU.is_equal),
                 reads=[("c", "iota16"), ("geidx", key)], writes=[("goh", key)])
            p.op("vector", lambda e: e.tensor_copy(ohb[:, :], oh[:, :]),
                 reads=[("goh", key)], writes=[("gohb", key)])
            p.op("vector", lambda e: e.tensor_scalar_mul(nmx[:, :], mx[:, :], -1.0),
                 reads=[("gmx", key)], writes=[("gnmx", key)])
            p.op("scalar", lambda e: e.activation(eg[:, :], gtok[:, :], AF.Exp,
                                                  bias=nmx[:, 0:1]),
                 reads=[("ggtok", key), ("gnmx", key)], writes=[("geg", key)])
            p.op("vector", lambda e: e.reduce_sum(seg[:, :], eg[:, :], axis=AX.X),
                 reads=[("geg", key)], writes=[("gseg", key)])
            p.op("vector", lambda e, tt=tt: e.reciprocal(probsb[:, tt:tt + 1], seg[:, :]),
                 reads=[("gseg", key)], writes=[("gprob", key, tt)])
            p.op("tensor", lambda e: e.matmul(PS[0:128, 5 * 512:5 * 512 + 16],
                                              triu[:, :], ohb[:, :], start=True, stop=False),
                 reads=[("c", "triu_incl"), ("gohb", key)], writes=[("ps", 5)])
            p.op("tensor", lambda e: e.matmul(PS[0:128, 5 * 512:5 * 512 + 16],
                                              o1x[:, :], offs[:, :], start=False, stop=True),
                 reads=[("c", "ones_1x128_f32"), ("goffs", key)], writes=[("ps", 5)])
            p.op("vector", lambda e: e.tensor_mul(rnk[:, :], PS[0:128, 5 * 512:5 * 512 + 16],
                                                  oh[:, :]),
                 reads=[("ps", 5), ("goh", key)], writes=[("grnk", key)])
            p.op("vector", lambda e: e.reduce_sum(rsel[:, :], rnk[:, :], axis=AX.X),
                 reads=[("grnk", key)], writes=[("grsel", key)])
            p.op("vector", lambda e, tt=tt: e.tensor_scalar_add(lranksb[:, tt:tt + 1],
                                                                rsel[:, :], -1.0),
                 reads=[("grsel", key)], writes=[("glrank", key, tt)])
            p.op("vector", lambda e, tt=tt: e.tensor_copy(eidxsb[:, tt:tt + 1], eidx[:, :]),
                 reads=[("geidx", key)], writes=[("geidxsb", key, tt)])
            p.op("tensor", lambda e: e.matmul(PS[0:1, 6 * 512 + 0:6 * 512 + 16],
                                              ocbf[:, :], ohb[:, :], start=True, stop=True),
                 reads=[("c", "ones_col_bf"), ("gohb", key)], writes=[("ps", 6)])
            p.op("vector", lambda e: e.tensor_add(offs[:, :], offs[:, :],
                                                  PS[0:1, 6 * 512 + 0:6 * 512 + 16]),
                 reads=[("goffs", key), ("ps", 6)], writes=[("goffs", key)])

    return gate_cb


def emit_epilogue_l1(ctx, x1fm, avfm, wob_l1, out1pfm, key):
    nc, p = ctx.nc, ctx.p
    PS = ctx.PS
    ctx.push_scope()
    wo = [ctx.salloc(f"e1_wo_{key}_{c}", [128, 1024], BF16) for c in range(8)]
    for c in range(8):
        p.op("sync", lambda e, c=c: e.dma_start(
            out=wo[c][:, :], in_=wob_l1[c * 128:(c + 1) * 128, :]),
            reads=[("cast", "wo1")], writes=[("e1wo", key, c)], dma=True)
    avt = [ctx.salloc(f"e1_av_{key}_{i}", [128, TS], BF16) for i in range(2)]
    xres = [ctx.salloc(f"e1_xres_{key}_{i}", [128, TS], F32) for i in range(2)]
    osum = [ctx.salloc(f"e1_osum_{key}_{i}", [128, TS], F32) for i in range(2)]
    for ts in range(NTS):
        for c in range(8):
            cb = c % 2
            p.op("sync", lambda e, c=c, ts=ts, cb=cb: e.dma_start(
                out=avt[cb][:, :], in_=avfm[c * 128:(c + 1) * 128, ts * TS:(ts + 1) * TS]),
                reads=[("avfm", key, 2 * c, ts), ("avfm", key, 2 * c + 1, ts)],
                writes=[("e1av", key, cb)], dma=True)
            for ds in range(8):
                p.op("tensor", lambda e, c=c, ds=ds, cb=cb: e.matmul(
                    PS[0:128, ds * 512:ds * 512 + TS],
                    wo[c][:, ds * 128:(ds + 1) * 128], avt[cb][:, :],
                    start=(c == 0), stop=(c == 7)),
                    reads=[("e1wo", key, c), ("e1av", key, cb)], writes=[("ps", ds)])
        for ds in range(8):
            db = ds % 2
            p.op("sync", lambda e, ds=ds, ts=ts, db=db: e.dma_start(
                out=xres[db][:, :], in_=x1fm[ds * 128:(ds + 1) * 128, ts * TS:(ts + 1) * TS]),
                reads=[("x1-src", key, ds, ts)], writes=[("e1xres", key, db)], dma=True)
            p.op("vector", lambda e, ds=ds, db=db: e.tensor_add(
                osum[db][:, :], PS[0:128, ds * 512:ds * 512 + TS], xres[db][:, :]),
                reads=[("ps", ds), ("e1xres", key, db)], writes=[("e1osum", key, db)])
            p.op("sync", lambda e, ds=ds, ts=ts, db=db: e.dma_start(
                out=out1pfm[ds * 128:(ds + 1) * 128, ts * TS:(ts + 1) * TS],
                in_=osum[db][:, :]),
                reads=[("e1osum", key, db)], writes=[("out1p", key, ds, ts)], dma=True)
    ctx.pop_scope()


def emit_layer1(ctx, x1_in, wq_in, wk_in, wv_in, wo_in, gw_in, out1p,
                xn1aug_out, route_out, counts_out):
    nc, p = ctx.nc, ctx.p
    PS = ctx.PS
    xnfm = nc.dram_tensor("xnfm1", [D, T], BF16).ap()
    qfm = nc.dram_tensor("qfm1", [D, T], BF16).ap()
    kfm = nc.dram_tensor("kfm1", [D, T], BF16).ap()
    avfm = nc.dram_tensor("avfm1", [D, T], BF16).ap()
    wqb = nc.dram_tensor("wqb1", [H * D, DH], BF16).ap()
    wkb = nc.dram_tensor("wkb1", [H * D, DH], BF16).ap()
    wvb = nc.dram_tensor("wvb1", [H * D, DH], BF16).ap()
    wob = nc.dram_tensor("wob1", [D, D], BF16).ap()
    for wsrc, dst, nm in [(wq_in, wqb, "wq1"), (wk_in, wkb, "wk1"),
                          (wv_in, wvb, "wv1"), (wo_in, wob, "wo1")]:
        p.op("gpsimd", lambda e, wsrc=wsrc, dst=dst: e.dma_start(out=dst, in_=wsrc),
             writes=[("cast", nm)], dma=True)

    key = ("l1",)
    ctx.push_scope()
    gw_s = ctx.salloc("gw_s", [16, D], F32)
    p.op("sync", lambda e: e.dma_start(out=gw_s[:, :], in_=gw_in),
         writes=[("gws",)], dma=True)
    gwT = [ctx.salloc(f"gwT_{c}", [128, 16], F32) for c in range(8)]
    for c in range(8):
        p.op("tensor", lambda e, c=c: e.transpose(
            PS[0:128, 5 * 512:5 * 512 + 16], gw_s[:, c * 128:(c + 1) * 128],
            ctx.c["ident_f32"][0:16, 0:16]),
            reads=[("gws",), ("c", "ident_f32")], writes=[("ps", 5)])
        p.op("vector", lambda e, c=c: e.tensor_copy(gwT[c][:, :], PS[0:128, 5 * 512:5 * 512 + 16]),
             reads=[("ps", 5)], writes=[("gwT", key, c)])
    srow = ctx.salloc("srow", [16, 1], F32)
    negs = ctx.salloc("negs", [1, 16], F32)
    p.op("vector", lambda e: e.reduce_sum(srow[:, :], gw_s[:, :], axis=AX.X),
         reads=[("gws",)], writes=[("gsrow",)])
    p.op("vector", lambda e: e.tensor_scalar_mul(srow[:, :], srow[:, :], -1.0),
         reads=[("gsrow",)], writes=[("gsrow",)])
    id16 = ctx.salloc("id16", [16, 16], F32)
    p.op("sync", lambda e: e.dma_start(out=id16[:, :],
                                       in_=ctx.c_dram["ident_f32"].ap()[0:16, 0:16]),
         writes=[("gid16",)], dma=True)
    p.op("tensor", lambda e: e.matmul(PS[0:1, 5 * 512:5 * 512 + 16], srow[:, :],
                                      id16[:, :], start=True, stop=True),
         reads=[("gsrow",), ("gid16",)], writes=[("ps", 5)])
    p.op("vector", lambda e: e.tensor_copy(negs[:, :], PS[0:1, 5 * 512:5 * 512 + 16]),
         reads=[("ps", 5)], writes=[("gnegs", key)])

    gate_bufs = (
        ctx.salloc("g_gsl", [16, TS], F32),
        ctx.salloc("g_gtok", [128, 16], F32),
        ctx.salloc("g_mx", [128, 1], F32),
        ctx.salloc("g_nmx", [128, 1], F32),
        ctx.salloc("g_oh", [128, 16], F32),
        ctx.salloc("g_cand", [128, 16], F32),
        ctx.salloc("g_eidx", [128, 1], F32),
        ctx.salloc("g_ohb", [128, 16], BF16),
        ctx.salloc("g_eg", [128, 16], F32),
        ctx.salloc("g_seg", [128, 1], F32),
        ctx.salloc("g_prob", [128, 1], F32),
        ctx.salloc("g_rnk", [128, 16], F32),
        ctx.salloc("g_rsel", [128, 1], F32),
        ctx.salloc("g_offs", [1, 16], F32),
        ctx.salloc("g_probsb", [128, 32], F32),
        ctx.salloc("g_eidxsb", [128, 32], F32),
        ctx.salloc("g_lranksb", [128, 32], F32),
        ctx.salloc("g_routet", [128, 4], F32),
    )
    gate_cb = emit_gate_router(ctx, gwT, negs, gate_bufs, key)
    emit_ln(ctx, x1_in, xnfm, key, gate_cb=gate_cb)
    offs = gate_bufs[13]
    probsb, eidxsb, lranksb, routet = gate_bufs[14], gate_bufs[15], gate_bufs[16], gate_bufs[17]
    p.op("sync", lambda e: e.dma_start(out=counts_out, in_=offs[:, :]),
         reads=[("goffs", key)], writes=[("countsout",)], dma=True)
    fence2 = p.op("gpsimd", lambda e: e.nop(),
                  reads=[("xnfm", key, c, q) for c in range(8) for q in range(8)],
                  writes=[("F2",)])
    for c in range(8):
        for q in range(8):
            p.lastw[("xnfm-src", key, c, q)] = fence2
            p.lastw[("xnfm-src2", key, c, q)] = fence2

    ctx.push_scope()
    s1b = len(p.ops)
    for tt in range(32):
        p.op("vector", lambda e, tt=tt: e.memset(routet[:, :], 0.0),
             writes=[("groutet",)])
        p.op("vector", lambda e, tt=tt: e.tensor_copy(routet[:, 0:1], eidxsb[:, tt:tt + 1]),
             reads=[("geidxsb", key, tt)], writes=[("groutet",)])
        p.op("vector", lambda e, tt=tt: e.tensor_copy(routet[:, 1:2], lranksb[:, tt:tt + 1]),
             reads=[("glrank", key, tt)], writes=[("groutet",)])
        p.op("vector", lambda e, tt=tt: e.tensor_copy(routet[:, 2:3], probsb[:, tt:tt + 1]),
             reads=[("gprob", key, tt)], writes=[("groutet",)])
        p.op("sync", lambda e, tt=tt: e.dma_start(
            out=route_out[tt * 128:(tt + 1) * 128, :], in_=routet[:, :]),
            reads=[("groutet",)], writes=[("routeout", tt)], dma=True)
    xtok = ctx.salloc("xtok", [128, 1024], BF16, is_transpose_dest=True)
    xaug = ctx.salloc("xaug", [128, 1032], BF16)
    for tt in range(32):
        p.op("sync", lambda e, tt=tt: e.dma_start_transpose(
            xtok[:, :], xnfm[0:D, tt * 128:(tt + 1) * 128]),
            reads=[("xnfm-src", key, 0, 0)], writes=[("xtok",)], dma=True)
        p.op("vector", lambda e: e.tensor_copy(xaug[:, 0:1024], xtok[:, :]),
             reads=[("xtok",)], writes=[("xaug",)])
        p.op("vector", lambda e: e.memset(xaug[:, 1024:1032], 0.0),
             writes=[("xaug",)])
        p.op("vector", lambda e, tt=tt: e.tensor_copy(xaug[:, 1024:1025],
                                                      probsb[:, tt:tt + 1]),
             reads=[("gprob", key, tt)], writes=[("xaug",)])
        p.op("sync", lambda e, tt=tt: e.dma_start(
            out=xn1aug_out[tt * 128:(tt + 1) * 128, :], in_=xaug[:, :]),
            reads=[("xaug",)], writes=[("xn1augout", tt)], dma=True)
    s2b = len(p.ops)

    def mk_l1src(wb, castnm):
        def f(c, tile, wkey):
            for h in range(H):
                p.op("sync", lambda e, c=c, h=h, tile=tile: e.dma_start(
                    out=tile[:, h * 64:(h + 1) * 64],
                    in_=wb[h * D + c * 128: h * D + (c + 1) * 128, :]),
                    reads=[("cast", castnm)], writes=[wkey], dma=True)
        return f

    ctx.push_scope()
    vptiles = [ctx.salloc(f"vp1_{tt}", [128, H * 65], BF16) for tt in range(T // 128)]
    emit_qkv(ctx, xnfm, qfm, kfm, mk_l1src(wqb, "wq1"), mk_l1src(wkb, "wk1"),
             mk_l1src(wvb, "wv1"), vptiles, key)
    p.interleave(s1b, s2b)
    fq = p.op("gpsimd", lambda e: e.nop(),
              reads=[("qfm", key, hp, ts2) for hp in range(8) for ts2 in range(8)],
              writes=[("F3",)])
    p.lastw[("qfm-all", key)] = fq
    fk = p.op("gpsimd", lambda e: e.nop(),
              reads=[("kfm", key, hp, ts2) for hp in range(8) for ts2 in range(8)],
              writes=[("F4",)])
    p.lastw[("kfm-all", key)] = fk
    emit_flash(ctx, qfm, kfm, vptiles, avfm, causal=False, key=key)
    ctx.pop_scope()
    ctx.pop_scope()

    emit_epilogue_l1(ctx, x1_in, avfm, wob, out1p, key)

    p.op("sync", lambda e: e.nop(),
         reads=[("out1p", key, ds, ts2) for ds in range(8) for ts2 in range(8)] +
               [("countsout",)] + [("routeout", tt) for tt in range(32)] +
               [("xn1augout", tt) for tt in range(32)],
         writes=[("END",)])


def build_launch_AB():
    nc = bass.Bass("TRN2", target_bir_lowering=False)
    ctx = Ctx(nc)
    ctx.c_dram = make_consts(nc)
    x_in = nc.dram_tensor("x", [T, D], F32, kind="ExternalInput").ap()
    wq_in = nc.dram_tensor("wq", [D, D], F32, kind="ExternalInput").ap()
    wk_in = nc.dram_tensor("wk", [D, D], F32, kind="ExternalInput").ap()
    wv_in = nc.dram_tensor("wv", [D, D], F32, kind="ExternalInput").ap()
    wo_in = nc.dram_tensor("wo", [H * D, DH], F32, kind="ExternalInput").ap()
    w1_in = nc.dram_tensor("w1", [FF, D], F32, kind="ExternalInput").ap()
    b1_in = nc.dram_tensor("b1", [32, 128], F32, kind="ExternalInput").ap()
    w2_in = nc.dram_tensor("w2", [D, FF], F32, kind="ExternalInput").ap()
    b2_in = nc.dram_tensor("b2", [1, D], F32, kind="ExternalInput").ap()
    wq1_in = nc.dram_tensor("wq1", [H * D, DH], F32, kind="ExternalInput").ap()
    wk1_in = nc.dram_tensor("wk1", [H * D, DH], F32, kind="ExternalInput").ap()
    wv1_in = nc.dram_tensor("wv1", [H * D, DH], F32, kind="ExternalInput").ap()
    wo1_in = nc.dram_tensor("wo1", [D, D], F32, kind="ExternalInput").ap()
    gw_in = nc.dram_tensor("gw", [NE, D], F32, kind="ExternalInput").ap()
    out1p = nc.dram_tensor("out1pfm", [D, T], F32, kind="ExternalOutput").ap()
    xn1aug_out = nc.dram_tensor("xn1aug", [T, 1032], BF16, kind="ExternalOutput").ap()
    route_out = nc.dram_tensor("route", [T, 4], F32, kind="ExternalOutput").ap()
    counts_out = nc.dram_tensor("counts", [1, 16], F32, kind="ExternalOutput").ap()
    x1fm = nc.dram_tensor("x1fm", [D, T], F32).ap()
    ctx.PS = nc.alloc_psum_tensor("PS", [128, 4096], F32)
    p = ctx.p
    load_consts(ctx, ["ident_bf", "ident_f32", "tri01_0", "tri01_1", "tri01_2",
                      "tri01_3", "ones_col_f32", "ones_1x128_f32",
                      "iota16", "triu_incl", "ones_col_bf"])
    emit_layer0(ctx, x_in, wq_in, wk_in, wv_in, wo_in, w1_in, b1_in, w2_in,
                b2_in, x1fm)
    fx1 = p.op("gpsimd", lambda e: e.nop(),
               reads=[("x1fm", ("l0",), ds, ts2) for ds in range(8) for ts2 in range(8)],
               writes=[("FX1",)])
    for c in range(8):
        for q in range(8):
            p.lastw[("xfm-src", ("l1",), c, q)] = fx1
            p.lastw[("x1-src", ("l1",), c, q)] = fx1
    emit_layer1(ctx, x1fm, wq1_in, wk1_in, wv1_in, wo1_in, gw_in, out1p,
                xn1aug_out, route_out, counts_out)
    p.finalize()
    return nc


def build_launch_C():
    nc = bass.Bass("TRN2", target_bir_lowering=False)
    ctx = Ctx(nc)
    ctx.c_dram = make_consts(nc)
    NTOK = NCORES * T
    xn_all = nc.dram_tensor("xn_all", [NTOK, 1032], BF16, kind="ExternalInput").ap()
    route_all = nc.dram_tensor("route_all", [NTOK, 4], F32, kind="ExternalInput").ap()
    counts_all = nc.dram_tensor("counts_all", [8, 16], F32, kind="ExternalInput").ap()
    w1e_in = nc.dram_tensor("w1e", [EPP * D, FF], F32, kind="ExternalInput").ap()
    w2e_in = nc.dram_tensor("w2e", [EPP * FF, D], F32, kind="ExternalInput").ap()
    mybase_in = nc.dram_tensor("mybase", [1, 1], F32, kind="ExternalInput").ap()
    eout = nc.dram_tensor("expert_my", [EPP * CAP, D], BF16, kind="ExternalOutput").ap()
    disp = nc.dram_tensor("disp", [EPP * CAP, 1032], BF16).ap()
    ctx.PS = nc.alloc_psum_tensor("PS", [128, 4096], F32)
    p = ctx.p
    load_consts(ctx, ["ident_f32", "ones_1x128_f32", "iota16", "striu8"])
    key = ("mc",)
    PS = ctx.PS

    ctx.push_scope()
    cts = ctx.salloc("cts", [8, 16], F32)
    offs8 = ctx.salloc("offs8", [8, 16], F32)
    p.op("sync", lambda e: e.dma_start(out=cts[:, :], in_=counts_all),
         writes=[("mcts",)], dma=True)
    st8 = ctx.salloc("st8", [8, 8], F32)
    p.op("sync", lambda e: e.dma_start(out=st8[:, :], in_=ctx.c_dram["striu8"].ap()),
         writes=[("mst8",)], dma=True)
    p.op("tensor", lambda e: e.matmul(PS[0:8, 0:16], st8[:, :], cts[:, :],
                                      start=True, stop=True),
         reads=[("mst8",), ("mcts",)], writes=[("ps", 0)])
    p.op("vector", lambda e: e.tensor_copy(offs8[:, :], PS[0:8, 0:16]),
         reads=[("ps", 0)], writes=[("moffs8",)])
    mybase = ctx.salloc("mybase_s", [1, 1], F32)
    p.op("sync", lambda e: e.dma_start(out=mybase[:, :], in_=mybase_in),
         writes=[("mybase",)], dma=True)
    mybB = ctx.salloc("mybB", [128, 1], F32)
    p.op("tensor", lambda e: e.matmul(PS[0:128, 512:513], ctx.c["ones_1x128_f32"][:, :],
                                      mybase[:, :], start=True, stop=True),
         reads=[("mybase",), ("c", "ones_1x128_f32")], writes=[("ps", 1)])
    p.op("vector", lambda e: e.tensor_copy(mybB[:, :], PS[0:128, 512:513]),
         reads=[("ps", 1)], writes=[("mybB",)])
    ident8 = ctx.salloc("ident8", [8, 8], F32)
    p.op("sync", lambda e: e.dma_start(out=ident8[:, :],
                                       in_=ctx.c_dram["ident_f32"].ap()[0:8, 0:8]),
         writes=[("mid8",)], dma=True)
    zt = ctx.salloc("zt", [128, 1032], BF16)
    p.op("vector", lambda e: e.memset(zt[:, :], 0.0), writes=[("mzt",)])
    for rt in range(EPP * CAP // 128):
        p.op("sync", lambda e, rt=rt: e.dma_start(
            out=disp[rt * 128:(rt + 1) * 128, :], in_=zt[:, :]),
            reads=[("mzt",)], writes=[("mdispz", rt)], dma=True)
    obc = ctx.salloc("obc", [128, 16], F32)
    NPAR = 2
    PENG = ["vector", "gpsimd"]
    rt_t = [ctx.salloc(f"rt_t{i}", [128, 4], F32) for i in range(NPAR)]
    ohx = [ctx.salloc(f"ohx{i}", [128, 16], F32) for i in range(NPAR)]
    offtok = [ctx.salloc(f"offtok{i}", [128, 1], F32) for i in range(NPAR)]
    grank = [ctx.salloc(f"grank{i}", [128, 1], F32) for i in range(NPAR)]
    valid = [ctx.salloc(f"valid{i}", [128, 1], F32) for i in range(NPAR)]
    badv = [ctx.salloc(f"badv{i}", [128, 1], F32) for i in range(NPAR)]
    destf = [ctx.salloc(f"destf{i}", [128, 1], F32) for i in range(NPAR)]
    desti = [ctx.salloc(f"desti{i}", [128, 1], I32) for i in range(NPAR)]
    xrow = [ctx.salloc(f"xrow{i}", [128, 1032], BF16) for i in range(NPAR)]
    bcreg = nc.alloc_register(mybir.EngineType.Pool, "bcreg")
    p.op("gpsimd", lambda e: e.reg_mov(bcreg, EPP * CAP), writes=[("bcreg",)])
    for cb in range(8):
        p.op("tensor", lambda e, cb=cb: e.matmul(
            PS[0:1, 0:16], ident8[:, cb:cb + 1], offs8[:, :], start=True, stop=True),
            reads=[("mid8",), ("moffs8",)], writes=[("ps", 0)])
        p.op("vector", lambda e: e.tensor_copy(obc[0:1, :], PS[0:1, 0:16]),
             reads=[("ps", 0)], writes=[("mobcrow",)])
        p.op("tensor", lambda e: e.matmul(PS[0:128, 0:16], ctx.c["ones_1x128_f32"][:, :],
                                          obc[0:1, :], start=True, stop=True),
             reads=[("mobcrow",), ("c", "ones_1x128_f32")], writes=[("ps", 0)])
        p.op("vector", lambda e: e.tensor_copy(obc[:, :], PS[0:128, 0:16]),
             reads=[("ps", 0)], writes=[("mobc",)])
        for tt in range(32):
            i = tt % NPAR
            E = PENG[i]
            base = cb * T + tt * 128
            p.op("sync", lambda e, base=base, i=i: e.dma_start(
                out=rt_t[i][:, :], in_=route_all[base:base + 128, :]),
                writes=[("mrt", i)], dma=True)
            p.op("vector", lambda e, i=i: e.tensor_scalar(ohx[i][:, :], ctx.c["iota16"][:, :],
                                                   rt_t[i][:, 0:1], None, ALU.is_equal),
                reads=[("mrt", i), ("c", "iota16")], writes=[("mohx", i)])
            p.op("vector", lambda e, i=i: e.tensor_mul(ohx[i][:, :], ohx[i][:, :], obc[:, :]),
                reads=[("mohx", i), ("mobc",)], writes=[("mohx", i)])
            p.op("vector", lambda e, i=i: e.reduce_sum(offtok[i][:, :], ohx[i][:, :], axis=AX.X),
                 reads=[("mohx", i)], writes=[("mofftok", i)])
            p.op(E, lambda e, i=i: e.tensor_add(grank[i][:, :], rt_t[i][:, 1:2], offtok[i][:, :]),
                 reads=[("mrt", i), ("mofftok", i)], writes=[("mgrank", i)])
            p.op(E, lambda e, i=i: e.tensor_scalar(valid[i][:, :], grank[i][:, :], float(CAP),
                                                   None, ALU.is_lt),
                 reads=[("mgrank", i)], writes=[("mvalid", i)])
            p.op(E, lambda e, i=i: e.tensor_scalar(destf[i][:, :], rt_t[i][:, 0:1], float(CAP),
                                                   None, ALU.mult),
                 reads=[("mrt", i)], writes=[("mdestf", i)])
            p.op(E, lambda e, i=i: e.tensor_add(destf[i][:, :], destf[i][:, :], grank[i][:, :]),
                 reads=[("mdestf", i), ("mgrank", i)], writes=[("mdestf", i)])
            p.op(E, lambda e, i=i: e.tensor_sub(destf[i][:, :], destf[i][:, :], mybB[:, :]),
                 reads=[("mdestf", i), ("mybB",)], writes=[("mdestf", i)])
            p.op(E, lambda e, i=i: e.tensor_scalar(badv[i][:, :], destf[i][:, :], 0.0,
                                                   None, ALU.is_ge),
                 reads=[("mdestf", i)], writes=[("mbadv", i)])
            p.op(E, lambda e, i=i: e.tensor_mul(valid[i][:, :], valid[i][:, :], badv[i][:, :]),
                 reads=[("mvalid", i), ("mbadv", i)], writes=[("mvalid", i)])
            p.op(E, lambda e, i=i: e.tensor_mul(destf[i][:, :], destf[i][:, :], valid[i][:, :]),
                 reads=[("mdestf", i), ("mvalid", i)], writes=[("mdestf", i)])
            p.op(E, lambda e, i=i: e.tensor_scalar(badv[i][:, :], valid[i][:, :], -1e9, 1e9,
                                                   ALU.mult, ALU.add),
                 reads=[("mvalid", i)], writes=[("mbadv", i)])
            p.op(E, lambda e, i=i: e.tensor_add(destf[i][:, :], destf[i][:, :], badv[i][:, :]),
                 reads=[("mdestf", i), ("mbadv", i)], writes=[("mdestf", i)])
            p.op(E, lambda e, i=i: e.tensor_copy(desti[i][:, :], destf[i][:, :]),
                 reads=[("mdestf", i)], writes=[("mdesti", i)])
            p.op("sync", lambda e, base=base, i=i: e.dma_start(
                out=xrow[i][:, :], in_=xn_all[base:base + 128, :]),
                writes=[("mxrow", i)], dma=True)
            p.op("gpsimd", lambda e, i=i: e.indirect_dma_start(
                disp, IndirectOffsetOnAxis(desti[i][:, 0:1], 0), xrow[i][:, :], None,
                bounds_check=bcreg, oob_is_err=False),
                reads=[("mdesti", i), ("mxrow", i), ("bcreg",)] +
                      ([("mdispz", rt) for rt in range(EPP * CAP // 128)] if cb == 0 and tt == 0 else []),
                writes=[("mdisp",)], dma=True)
    fence = p.op("gpsimd", lambda e: e.nop(), reads=[("mdisp",)], writes=[("mdispF",)])
    ctx.pop_scope()

    ctx.push_scope()
    dispT = [ctx.salloc(f"dispT_{c}", [128, CAP], BF16, is_transpose_dest=True)
             for c in range(8)]
    w1r = [ctx.salloc(f"w1r_{c}", [128, FF], BF16) for c in range(8)]
    probc = ctx.salloc("probc", [128, 8], BF16)
    w2b = [ctx.salloc(f"w2b_{i}", [128, 1024], BF16) for i in range(4)]
    hsl = [ctx.salloc(f"mh_{f}", [128, TS], BF16) for f in range(32)]
    gs = ctx.salloc("gs", [128, TS], F32)
    gu = ctx.salloc("gu", [128, TS], F32)
    gt = ctx.salloc("gt", [128, TS], F32)
    gw_ = ctx.salloc("gw_", [128, TS], F32)
    ga = ctx.salloc("ga", [128, TS], F32)
    outb = [ctx.salloc(f"outb_{i}", [128, TS], BF16) for i in range(8)]
    probv = [ctx.salloc(f"probv_{i}", [128, 1], F32) for i in range(4)]
    C0 = 0.044715
    C1 = math.sqrt(2.0 / math.pi)
    for ex in range(EPP):
        for c in range(8):
            p.op("sync", lambda e, ex=ex, c=c: e.dma_start_transpose(
                dispT[c][:, :], disp[ex * CAP:(ex + 1) * CAP, c * 128:(c + 1) * 128]),
                extra_deps=[fence], writes=[("mdispT", c)], dma=True)
            p.op("gpsimd", lambda e, ex=ex, c=c: e.dma_start(
                out=w1r[c][:, :], in_=w1e_in[ex * D + c * 128:ex * D + (c + 1) * 128, :]),
                writes=[("mw1r", c)], dma=True)
        for capb in range(CAP // TS):
            for fc in range(32):
                bank = 4 + fc % 2
                for c in range(8):
                    p.op("tensor", lambda e, c=c, fc=fc, capb=capb, bank=bank: e.matmul(
                        PS[0:128, bank * 512:bank * 512 + TS],
                        w1r[c][:, fc * 128:(fc + 1) * 128],
                        dispT[c][:, capb * TS:(capb + 1) * TS],
                        start=(c == 0), stop=(c == 7)),
                        reads=[("mw1r", c), ("mdispT", c)], writes=[("ps", bank)])
                p.op("scalar", lambda e, bank=bank: e.activation(
                    gs[:, :], PS[0:128, bank * 512:bank * 512 + TS], AF.Square),
                    reads=[("ps", bank)], writes=[("mgs",)])
                p.op("vector", lambda e: e.tensor_scalar(gu[:, :], gs[:, :], C0, 1.0,
                                                         ALU.mult, ALU.add),
                     reads=[("mgs",)], writes=[("mgu",)])
                p.op("vector", lambda e, bank=bank: e.tensor_mul(
                    gt[:, :], gu[:, :], PS[0:128, bank * 512:bank * 512 + TS]),
                    reads=[("mgu",), ("ps", bank)], writes=[("mgt",)])
                p.op("scalar", lambda e: e.activation(gw_[:, :], gt[:, :], AF.Tanh,
                                                      scale=C1),
                     reads=[("mgt",)], writes=[("mgw",)])
                p.op("vector", lambda e: e.tensor_scalar(ga[:, :], gw_[:, :], 0.5, 0.5,
                                                         ALU.mult, ALU.add),
                     reads=[("mgw",)], writes=[("mga",)])
                p.op("vector", lambda e, fc=fc, bank=bank: e.tensor_mul(
                    hsl[fc][:, :], ga[:, :], PS[0:128, bank * 512:bank * 512 + TS]),
                    reads=[("mga",), ("ps", bank)], writes=[("mh", fc)])
            for sub in range(4):
                ct = capb * 4 + sub
                p.op("sync", lambda e, ex=ex, ct=ct, sub=sub: e.dma_start(
                    out=probc[:, :], in_=disp[ex * CAP + ct * 128:ex * CAP + (ct + 1) * 128,
                                              1024:1032]),
                    extra_deps=[fence], writes=[("mprobc",)], dma=True)
                p.op("vector", lambda e, sub=sub: e.tensor_copy(probv[sub][:, :], probc[:, 0:1]),
                     reads=[("mprobc",)], writes=[("mprobv", sub)])
            for fc in range(32):
                wi = fc % 4
                p.op("gpsimd", lambda e, ex=ex, fc=fc, wi=wi: e.dma_start(
                    out=w2b[wi][:, :],
                    in_=w2e_in[ex * FF + fc * 128:ex * FF + (fc + 1) * 128, :]),
                    writes=[("mw2b", wi)], dma=True)
                for sub in range(4):
                    for half in range(2):
                        bank = sub * 2 + half
                        p.op("tensor", lambda e, fc=fc, sub=sub, half=half, bank=bank, wi=wi: e.matmul(
                            PS[0:128, bank * 512:bank * 512 + TS],
                            hsl[fc][:, sub * 128:(sub + 1) * 128],
                            w2b[wi][:, half * 512:(half + 1) * 512],
                            start=(fc == 0), stop=(fc == 31)),
                            reads=[("mh", fc), ("mw2b", wi)], writes=[("ps", bank)])
            for sub in range(4):
                ct = capb * 4 + sub
                for half in range(2):
                    bank = sub * 2 + half
                    p.op("vector", lambda e, sub=sub, half=half, bank=bank: e.tensor_scalar(
                        outb[sub * 2 + half][:, :],
                        PS[0:128, bank * 512:bank * 512 + TS],
                        probv[sub][:, 0:1], None, ALU.mult),
                        reads=[("ps", bank), ("mprobv", sub)], writes=[("moutb", sub, half)])
                p.op("sync", lambda e, ex=ex, ct=ct, sub=sub: e.dma_start(
                    out=eout[ex * CAP + ct * 128:ex * CAP + (ct + 1) * 128, 0:512],
                    in_=outb[sub * 2][:, :]),
                    reads=[("moutb", sub, 0)], writes=[("meout", ex, ct)], dma=True)
                p.op("sync", lambda e, ex=ex, ct=ct, sub=sub: e.dma_start(
                    out=eout[ex * CAP + ct * 128:ex * CAP + (ct + 1) * 128, 512:1024],
                    in_=outb[sub * 2 + 1][:, :]),
                    reads=[("moutb", sub, 1)], writes=[("meout", ex, ct)], dma=True)
    ctx.pop_scope()
    p.op("sync", lambda e: e.nop(),
         reads=[("meout", ex, ct) for ex in range(EPP) for ct in range(CAP // 128)],
         writes=[("END",)])
    p.finalize()
    return nc


def build_launch_D():
    nc = bass.Bass("TRN2", target_bir_lowering=False)
    ctx = Ctx(nc)
    ctx.c_dram = make_consts(nc)
    NSLOT = NE * CAP
    expert_all = nc.dram_tensor("expert_all", [NSLOT, D], BF16, kind="ExternalInput").ap()
    route_in = nc.dram_tensor("route", [T, 4], F32, kind="ExternalInput").ap()
    counts_all = nc.dram_tensor("counts_all", [8, 16], F32, kind="ExternalInput").ap()
    out1p_in = nc.dram_tensor("out1pfm", [D, T], F32, kind="ExternalInput").ap()
    ecore_in = nc.dram_tensor("ecore", [8, 1], F32, kind="ExternalInput").ap()
    out_tok = nc.dram_tensor("out", [T, D], F32, kind="ExternalOutput").ap()
    moetok = nc.dram_tensor("moetok", [T, D], BF16).ap()
    out2fm = nc.dram_tensor("out2fm", [D, T], F32).ap()
    ctx.PS = nc.alloc_psum_tensor("PS", [128, 3584], F32)
    p = ctx.p
    load_consts(ctx, ["ident_f32", "ones_1x128_f32", "iota16", "striu8"])
    PS = ctx.PS
    key = ("md",)

    ctx.push_scope()
    cts = ctx.salloc("cts", [8, 16], F32)
    st8 = ctx.salloc("st8", [8, 8], F32)
    ecore = ctx.salloc("ecore_s", [8, 1], F32)
    obc = ctx.salloc("obc", [128, 16], F32)
    p.op("sync", lambda e: e.dma_start(out=cts[:, :], in_=counts_all),
         writes=[("dcts",)], dma=True)
    p.op("sync", lambda e: e.dma_start(out=st8[:, :], in_=ctx.c_dram["striu8"].ap()),
         writes=[("dst8",)], dma=True)
    p.op("sync", lambda e: e.dma_start(out=ecore[:, :], in_=ecore_in),
         writes=[("dec",)], dma=True)
    p.op("tensor", lambda e: e.matmul(PS[0:8, 0:16], st8[:, :], cts[:, :],
                                      start=True, stop=True),
         reads=[("dst8",), ("dcts",)], writes=[("ps", 0)])
    offs8 = ctx.salloc("offs8", [8, 16], F32)
    p.op("vector", lambda e: e.tensor_copy(offs8[:, :], PS[0:8, 0:16]),
         reads=[("ps", 0)], writes=[("doffs8",)])
    p.op("tensor", lambda e: e.matmul(PS[0:1, 0:16], ecore[:, :], offs8[:, :],
                                      start=True, stop=True),
         reads=[("dec",), ("doffs8",)], writes=[("ps", 0)])
    myoff = ctx.salloc("myoff", [1, 16], F32)
    p.op("vector", lambda e: e.tensor_copy(myoff[:, :], PS[0:1, 0:16]),
         reads=[("ps", 0)], writes=[("dmyoff",)])
    p.op("tensor", lambda e: e.matmul(PS[0:128, 0:16], ctx.c["ones_1x128_f32"][:, :],
                                      myoff[:, :], start=True, stop=True),
         reads=[("dmyoff",), ("c", "ones_1x128_f32")], writes=[("ps", 0)])
    p.op("vector", lambda e: e.tensor_copy(obc[:, :], PS[0:128, 0:16]),
         reads=[("ps", 0)], writes=[("dobc",)])
    bcreg = nc.alloc_register(mybir.EngineType.Pool, "bcreg")
    p.op("gpsimd", lambda e: e.reg_mov(bcreg, NSLOT), writes=[("bcreg",)])
    NPAR = 2
    PENG = ["vector", "gpsimd"]
    rt_t = [ctx.salloc(f"rt_t{i}", [128, 4], F32) for i in range(NPAR)]
    ohx = [ctx.salloc(f"ohx{i}", [128, 16], F32) for i in range(NPAR)]
    offtok = [ctx.salloc(f"offtok{i}", [128, 1], F32) for i in range(NPAR)]
    grank = [ctx.salloc(f"grank{i}", [128, 1], F32) for i in range(NPAR)]
    valid = [ctx.salloc(f"valid{i}", [128, 1], F32) for i in range(NPAR)]
    badv = [ctx.salloc(f"dbadv{i}", [128, 1], F32) for i in range(NPAR)]
    srcf = [ctx.salloc(f"srcf{i}", [128, 1], F32) for i in range(NPAR)]
    srci = [ctx.salloc(f"srci{i}", [128, 1], I32) for i in range(NPAR)]
    mrow = [ctx.salloc(f"mrow{i}", [128, D], BF16) for i in range(NPAR)]
    for tt in range(32):
        i = tt % NPAR
        E = PENG[i]
        p.op("sync", lambda e, tt=tt, i=i: e.dma_start(
            out=rt_t[i][:, :], in_=route_in[tt * 128:(tt + 1) * 128, :]),
            writes=[("drt", i)], dma=True)
        p.op("vector", lambda e, i=i: e.tensor_scalar(ohx[i][:, :], ctx.c["iota16"][:, :],
                                               rt_t[i][:, 0:1], None, ALU.is_equal),
            reads=[("drt", i), ("c", "iota16")], writes=[("dohx", i)])
        p.op("vector", lambda e, i=i: e.tensor_mul(ohx[i][:, :], ohx[i][:, :], obc[:, :]),
            reads=[("dohx", i), ("dobc",)], writes=[("dohx", i)])
        p.op("vector", lambda e, i=i: e.reduce_sum(offtok[i][:, :], ohx[i][:, :], axis=AX.X),
             reads=[("dohx", i)], writes=[("dofftok", i)])
        p.op(E, lambda e, i=i: e.tensor_add(grank[i][:, :], rt_t[i][:, 1:2], offtok[i][:, :]),
             reads=[("drt", i), ("dofftok", i)], writes=[("dgrank", i)])
        p.op(E, lambda e, i=i: e.tensor_scalar(valid[i][:, :], grank[i][:, :], float(CAP),
                                               None, ALU.is_lt),
             reads=[("dgrank", i)], writes=[("dvalid", i)])
        p.op(E, lambda e, i=i: e.tensor_scalar(srcf[i][:, :], rt_t[i][:, 0:1], float(CAP),
                                               None, ALU.mult),
             reads=[("drt", i)], writes=[("dsrcf", i)])
        p.op(E, lambda e, i=i: e.tensor_add(srcf[i][:, :], srcf[i][:, :], grank[i][:, :]),
             reads=[("dsrcf", i), ("dgrank", i)], writes=[("dsrcf", i)])
        p.op(E, lambda e, i=i: e.tensor_mul(srcf[i][:, :], srcf[i][:, :], valid[i][:, :]),
             reads=[("dsrcf", i), ("dvalid", i)], writes=[("dsrcf", i)])
        p.op(E, lambda e, i=i: e.tensor_scalar(badv[i][:, :], valid[i][:, :], -1e9, 1e9,
                                               ALU.mult, ALU.add),
             reads=[("dvalid", i)], writes=[("dbadv", i)])
        p.op(E, lambda e, i=i: e.tensor_add(srcf[i][:, :], srcf[i][:, :], badv[i][:, :]),
             reads=[("dsrcf", i), ("dbadv", i)], writes=[("dsrcf", i)])
        p.op(E, lambda e, i=i: e.tensor_copy(srci[i][:, :], srcf[i][:, :]),
             reads=[("dsrcf", i)], writes=[("dsrci", i)])
        p.op("vector", lambda e, i=i: e.memset(mrow[i][:, :], 0.0), writes=[("dmrow", i)])
        p.op("gpsimd", lambda e, i=i: e.indirect_dma_start(
            mrow[i][:, :], None, expert_all, IndirectOffsetOnAxis(srci[i][:, 0:1], 0),
            bounds_check=bcreg, oob_is_err=False),
            reads=[("dsrci", i), ("dmrow", i), ("bcreg",)], writes=[("dmrow", i)], dma=True)
        p.op("sync", lambda e, tt=tt, i=i: e.dma_start(
            out=moetok[tt * 128:(tt + 1) * 128, :], in_=mrow[i][:, :]),
            reads=[("dmrow", i)], writes=[("dmoetok", tt)], dma=True)
    fence = p.op("gpsimd", lambda e: e.nop(),
                 reads=[("dmoetok", tt) for tt in range(32)], writes=[("dmoeF",)])
    ctx.pop_scope()

    ctx.push_scope()
    moefm = ctx.salloc("moefm", [128, T], BF16, is_transpose_dest=True)
    x1t = ctx.salloc("x1t", [128, TS], F32)
    osum = ctx.salloc("osum", [128, TS], F32)
    for c in range(8):
        p.op("sync", lambda e, c=c: e.dma_start_transpose(
            moefm[:, :], moetok[0:T, c * 128:(c + 1) * 128]),
            extra_deps=[fence], writes=[("dmoefm",)], dma=True)
        for ts in range(NTS):
            p.op("sync", lambda e, c=c, ts=ts: e.dma_start(
                out=x1t[:, :], in_=out1p_in[c * 128:(c + 1) * 128, ts * TS:(ts + 1) * TS]),
                writes=[("dx1t",)], dma=True)
            p.op("vector", lambda e, ts=ts: e.tensor_add(
                osum[:, :], x1t[:, :], moefm[:, ts * TS:(ts + 1) * TS]),
                reads=[("dx1t",), ("dmoefm",)], writes=[("dosum",)])
            p.op("sync", lambda e, c=c, ts=ts: e.dma_start(
                out=out2fm[c * 128:(c + 1) * 128, ts * TS:(ts + 1) * TS], in_=osum[:, :]),
                reads=[("dosum",)], writes=[("dout2", c, ts)], dma=True)
    ctx.pop_scope()
    f2 = p.op("gpsimd", lambda e: e.nop(),
              reads=[("dout2", c, ts) for c in range(8) for ts in range(8)],
              writes=[("dout2F",)])
    for c in range(8):
        p.lastw[("f2t-src", ("o",), c)] = f2
    emit_fm_to_tok_f32(ctx, out2fm, out_tok, ("o",))
    p.op("sync", lambda e: e.nop(),
         reads=[("f2t-out", ("o",), g, c) for g in range(8) for c in range(8)],
         writes=[("END",)])
    p.finalize()
    return nc


def emit_layer0(ctx, x_in, wq_in, wk_in, wv_in, wo_in, w1_in, b1_in, w2_in,
                b2_in, x1fm_out):
    nc, p = ctx.nc, ctx.p
    xfm = nc.dram_tensor("xfm", [D, T], F32).ap()
    xnfm = nc.dram_tensor("xnfm", [D, T], BF16).ap()
    qfm = nc.dram_tensor("qfm", [D, T], BF16).ap()
    kfm = nc.dram_tensor("kfm", [D, T], BF16).ap()
    avfm = nc.dram_tensor("avfm", [D, T], BF16).ap()
    wqb = nc.dram_tensor("wqb", [D, D], BF16).ap()
    wkb = nc.dram_tensor("wkb", [D, D], BF16).ap()
    wvb = nc.dram_tensor("wvb", [D, D], BF16).ap()
    w1qb = nc.dram_tensor("w1qb", [FF, D], BF16).ap()
    w2qb = nc.dram_tensor("w2qb", [D, FF], BF16).ap()
    w2qtb = nc.dram_tensor("w2qtb", [FF, D], BF16).ap()

    for wsrc, dst, nm in [(wq_in, wqb, "wq"), (wk_in, wkb, "wk"), (wv_in, wvb, "wv")]:
        p.op("gpsimd", lambda e, wsrc=wsrc, dst=dst: e.dma_start(out=dst, in_=wsrc),
             writes=[("cast", nm)], dma=True)

    ctx.push_scope()
    s1 = len(p.ops)
    emit_quantize(ctx, w1_in, FF, D, w1qb, ("l0w1",), scoped=False)
    emit_quantize(ctx, w2_in, D, FF, w2qb, ("l0w2",), wqT_dram=w2qtb, scoped=False)
    s2 = len(p.ops)
    emit_x_to_fm(ctx, x_in, xfm, ("x2f",))
    fence = p.op("gpsimd", lambda e: e.nop(),
                 reads=[("xfm", ("x2f",), c, g) for c in range(8) for g in range(8)],
                 writes=[("F1",)])
    for c in range(8):
        for q in range(8):
            p.lastw[("xfm-src", ("l0",), c, q)] = fence
            p.lastw[("xfm-src3", ("l0",), c, q)] = fence

    emit_ln(ctx, xfm, xnfm, ("l0",))
    p.interleave(s1, s2)
    ctx.pop_scope()
    fence2 = p.op("gpsimd", lambda e: e.nop(),
                  reads=[("xnfm", ("l0",), c, q) for c in range(8) for q in range(8)],
                  writes=[("F2",)])
    for c in range(8):
        for q in range(8):
            p.lastw[("xnfm-src", ("l0",), c, q)] = fence2
            p.lastw[("xnfm-src2", ("l0",), c, q)] = fence2

    def mk_tsrc(wb, castnm):
        def f(c, tile, wkey):
            p.op("sync", lambda e, c=c, tile=tile: e.dma_start_transpose(
                tile[:, :], wb[0:D, c * 128:(c + 1) * 128]),
                reads=[("cast", castnm)], writes=[wkey], dma=True)
        return f

    ctx.push_scope()
    vptiles = [ctx.salloc(f"vp_{tt}", [128, H * 65], BF16) for tt in range(T // 128)]
    emit_qkv(ctx, xnfm, qfm, kfm, mk_tsrc(wqb, "wq"), mk_tsrc(wkb, "wk"),
             mk_tsrc(wvb, "wv"), vptiles, ("l0",))
    fq = p.op("gpsimd", lambda e: e.nop(),
              reads=[("qfm", ("l0",), hp, ts2) for hp in range(8) for ts2 in range(8)],
              writes=[("F3",)])
    p.lastw[("qfm-all", ("l0",))] = fq
    fk = p.op("gpsimd", lambda e: e.nop(),
              reads=[("kfm", ("l0",), hp, ts2) for hp in range(8) for ts2 in range(8)],
              writes=[("F4",)])
    p.lastw[("kfm-all", ("l0",))] = fk

    emit_flash(ctx, qfm, kfm, vptiles, avfm, causal=True, key=("l0",))
    ctx.pop_scope()

    emit_epilogue_l0(ctx, xfm, xnfm, avfm, wo_in, w1qb, w2qtb, b1_in, b2_in,
                     x1fm_out, ("l0",))

    p.op("sync", lambda e: e.nop(),
         reads=[("x1fm", ("l0",), ds2, ts2) for ds2 in range(8) for ts2 in range(8)],
         writes=[("ENDL0",)])


_launches = {}


def _get(name, builder):
    if name not in _launches:
        _launches[name] = builder()
    return _launches[name]


def kernel(L0_attn_x, L0_attn_WQ, L0_attn_WK, L0_attn_WV, L0_attn_WO,
           L0_ffn_W1, L0_ffn_b1, L0_ffn_W2, L0_ffn_b2,
           L1_attn_W_q, L1_attn_W_k, L1_attn_W_v, L1_attn_W_o,
           L1_ffn_gate_w, L1_ffn_w1, L1_ffn_w2):
    cores = list(range(NCORES))
    x = np.ascontiguousarray(np.asarray(L0_attn_x, dtype=np.float32))
    wq = np.ascontiguousarray(np.asarray(L0_attn_WQ, np.float32).reshape(D, D))
    wk = np.ascontiguousarray(np.asarray(L0_attn_WK, np.float32).reshape(D, D))
    wv = np.ascontiguousarray(np.asarray(L0_attn_WV, np.float32).reshape(D, D))
    wo = np.ascontiguousarray(np.asarray(L0_attn_WO, np.float32).reshape(H * D, DH))
    w1 = np.ascontiguousarray(np.asarray(L0_ffn_W1, np.float32))
    b1 = np.ascontiguousarray(np.asarray(L0_ffn_b1, np.float32).reshape(32, 128))
    w2 = np.ascontiguousarray(np.asarray(L0_ffn_W2, np.float32))
    b2 = np.ascontiguousarray(np.asarray(L0_ffn_b2, np.float32).reshape(1, D))
    wq1 = np.ascontiguousarray(np.asarray(L1_attn_W_q, np.float32).reshape(H * D, DH))
    wk1 = np.ascontiguousarray(np.asarray(L1_attn_W_k, np.float32).reshape(H * D, DH))
    wv1 = np.ascontiguousarray(np.asarray(L1_attn_W_v, np.float32).reshape(H * D, DH))
    wo1 = np.ascontiguousarray(np.asarray(L1_attn_W_o, np.float32))
    gw = np.ascontiguousarray(np.asarray(L1_ffn_gate_w, np.float32))
    w1e = np.asarray(L1_ffn_w1, np.float32)
    w2e = np.asarray(L1_ffn_w2, np.float32)

    lAB = _get("AB", build_launch_AB)
    in_maps = [{"x": x[c], "wq": wq, "wk": wk, "wv": wv, "wo": wo,
                "w1": w1, "b1": b1, "w2": w2, "b2": b2,
                "wq1": wq1, "wk1": wk1, "wv1": wv1, "wo1": wo1, "gw": gw}
               for c in cores]
    resB = run_bass_kernel_spmd(lAB, in_maps, core_ids=cores)
    out1p = [np.ascontiguousarray(resB.results[c]["out1pfm"]) for c in cores]
    xn_all = np.ascontiguousarray(
        np.concatenate([resB.results[c]["xn1aug"] for c in cores], axis=0))
    route_all = np.ascontiguousarray(
        np.concatenate([resB.results[c]["route"] for c in cores], axis=0))
    counts_all = np.ascontiguousarray(
        np.concatenate([resB.results[c]["counts"] for c in cores], axis=0))

    lC = _get("C", build_launch_C)
    in_maps = []
    for c in cores:
        in_maps.append({
            "xn_all": xn_all, "route_all": route_all, "counts_all": counts_all,
            "w1e": np.ascontiguousarray(w1e[EPP * c:EPP * (c + 1)].reshape(EPP * D, FF)),
            "w2e": np.ascontiguousarray(w2e[EPP * c:EPP * (c + 1)].reshape(EPP * FF, D)),
            "mybase": np.full((1, 1), float(c * EPP * CAP), np.float32),
        })
    resC = run_bass_kernel_spmd(lC, in_maps, core_ids=cores)
    expert_all = np.ascontiguousarray(
        np.concatenate([resC.results[c]["expert_my"] for c in cores], axis=0))

    lD = _get("D", build_launch_D)
    in_maps = []
    for c in cores:
        ec = np.zeros((8, 1), np.float32)
        ec[c, 0] = 1.0
        in_maps.append({"expert_all": expert_all,
                        "route": np.ascontiguousarray(resB.results[c]["route"]),
                        "counts_all": counts_all, "out1pfm": out1p[c],
                        "ecore": ec})
    resD = run_bass_kernel_spmd(lD, in_maps, core_ids=cores)
    out = np.stack([resD.results[c]["out"] for c in cores])
    return out.astype(np.float32)


# version 07 (final)
# speedup vs baseline: 1.0278x; 1.0278x over previous
import math
import numpy as np
import ml_dtypes
import concourse.bass as bass
import concourse.mybir as mybir
from concourse.bass_utils import run_bass_kernel_spmd
from concourse.bass_core_rust import IndirectOffsetOnAxis

F32 = mybir.dt.float32
BF16 = mybir.dt.bfloat16
I32 = mybir.dt.int32
AF = mybir.ActivationFunctionType
ALU = mybir.AluOpType
AX = mybir.AxisListType
bf = ml_dtypes.bfloat16

NCORES = 8
T = 4096
D = 1024
H = 16
DH = 64
FF = 4096
NE = 16
CAP = 2560
EPP = 2
TS = 512
NTS = T // TS

ENGINE_ORDER = ["sync", "gpsimd", "vector", "scalar", "tensor"]


class Prog:

    NLANES = 8

    def __init__(self, nc):
        self.nc = nc
        self.ops = []
        self.lastw = {}
        self.readers = {}
        self.last_on_engine = {}
        self.dma_count = {}
        self._opkeys_r = []
        self._opkeys_w = []

    def op(self, engine, emit, reads=(), writes=(), dma=False, tag="", extra_deps=()):
        i = len(self.ops)
        deps = set(extra_deps)
        for k in reads:
            if k in self.lastw:
                deps.add(self.lastw[k])
        for k in writes:
            if k in self.lastw:
                deps.add(self.lastw[k])
            for r in self.readers.get(k, ()):
                deps.add(r)
        deps.discard(i)
        lane = None
        if dma:
            idx = self.dma_count.get(engine, 0)
            self.dma_count[engine] = idx + 1
            lane = idx % self.NLANES
        self.ops.append(dict(engine=engine, emit=emit, deps=deps,
                             inc=16 if dma else 1, tag=tag, lane=lane))
        self._opkeys_r.append(tuple(reads))
        self._opkeys_w.append(tuple(writes))
        for k in reads:
            self.readers.setdefault(k, []).append(i)
        for k in writes:
            self.lastw[k] = i
            self.readers[k] = []
        self.last_on_engine[engine] = i
        return i

    def interleave(self, s1, s2):
        end = len(self.ops)
        A = list(range(s1, s2))
        B = list(range(s2, end))
        if not A or not B:
            return
        aw = set()
        ar = set()
        for i in A:
            aw.update(k for k in self._opkeys_w[i])
            ar.update(k for k in self._opkeys_r[i])
        for i in B:
            bw = set(self._opkeys_w[i])
            br = set(self._opkeys_r[i])
            assert not (aw & bw) and not (aw & br) and not (ar & bw), (
                f"interleave ranges share keys: {sorted((aw & bw) | (aw & br) | (ar & bw))[:6]}")
        merged = []
        ia = ib = 0
        ratio = len(A) / (len(A) + len(B))
        na = nb = 0
        while ia < len(A) or ib < len(B):
            take_a = ib >= len(B) or (ia < len(A) and (na + nb == 0 or na / (na + nb) <= ratio))
            if take_a:
                merged.append(A[ia]); ia += 1; na += 1
            else:
                merged.append(B[ib]); ib += 1; nb += 1
        perm = list(range(s1)) + merged
        inv = {old: new for new, old in enumerate(perm)}
        self.ops = [self.ops[old] for old in perm]
        self._opkeys_r = [self._opkeys_r[old] for old in perm]
        self._opkeys_w = [self._opkeys_w[old] for old in perm]
        for o in self.ops:
            o["deps"] = {inv[d] for d in o["deps"]}
        self.lastw = {k: inv[v] for k, v in self.lastw.items()}
        self.readers = {k: [inv[v] for v in vs] for k, vs in self.readers.items()}
        self.last_on_engine = {}
        self.dma_count = {}
        for i, o in enumerate(self.ops):
            self.last_on_engine[o["engine"]] = i
            if o["lane"] is not None:
                idx = self.dma_count.get(o["engine"], 0)
                self.dma_count[o["engine"]] = idx + 1
                o["lane"] = idx % self.NLANES

    def hard_fence(self):
        snap = dict(self.last_on_engine)
        for e in ENGINE_ORDER:
            self.op(e, lambda en: en.nop(), extra_deps=set(snap.values()),
                    tag="fence")

    def range_fence(self, start):
        deps = {}
        for i in range(start, len(self.ops)):
            o = self.ops[i]
            k = (o["engine"], o["lane"])
            deps[k] = i
        dep_set = set(deps.values())
        if not dep_set:
            return
        for e in ENGINE_ORDER:
            self.op(e, lambda en: en.nop(), extra_deps=dep_set, tag="rfence")

    def finalize(self):
        nc = self.nc
        n = len(self.ops)
        ops = self.ops
        need = [False] * n
        for i, o in enumerate(ops):
            if o["inc"] == 16:
                need[i] = True
            for d in o["deps"]:
                if (ops[d]["engine"] == "tensor" and o["engine"] == "tensor"
                        and ops[d]["inc"] == 1):
                    continue
                need[d] = True
        engines = sorted({o["engine"] for o in ops},
                         key=ENGINE_ORDER.index)
        esem = {e: nc.alloc_semaphore(f"S_{e}") for e in engines}
        dsem = {}
        for e in sorted(self.dma_count):
            for l in range(self.NLANES):
                dsem[(e, l)] = nc.alloc_semaphore(f"D_{e}_{l}")
        token = [None] * n
        ecnt = {}
        lanecnt = {}
        lane_prev = [0] * n
        for i, o in enumerate(ops):
            e = o["engine"]
            if o["lane"] is not None:
                lk = (e, o["lane"])
                c = lanecnt.get(lk, 0)
                lane_prev[i] = c
                c += 16
                lanecnt[lk] = c
                token[i] = (("d", e, o["lane"]), c)
            else:
                c = ecnt.get(e, 0)
                if need[i]:
                    c += 1
                ecnt[e] = c
                token[i] = (("e", e), c)

        def semof(sk):
            return dsem[(sk[1], sk[2])] if sk[0] == "d" else esem[sk[1]]

        with nc.Block() as block:
            for e in engines:
                ops_e = [(i, o) for i, o in enumerate(ops) if o["engine"] == e]

                def body(eng, e=e, ops_e=ops_e):
                    waited = {}
                    for i, o in ops_e:
                        wv = {}
                        for d in o["deps"]:
                            de = ops[d]["engine"]
                            if de == "tensor" and e == "tensor" and ops[d]["inc"] == 1:
                                continue
                            sk, v = token[d]
                            if sk[0] == "e" and not need[d]:
                                sk, v = None, 0
                            if sk is not None and v > wv.get(sk, -1):
                                wv[sk] = v
                        if o["lane"] is not None and lane_prev[i] > 0:
                            sk = ("d", e, o["lane"])
                            if lane_prev[i] > wv.get(sk, -1):
                                wv[sk] = lane_prev[i]
                        for sk in sorted(wv):
                            v = wv[sk]
                            if v > waited.get(sk, 0):
                                eng.wait_ge(semof(sk), v)
                                waited[sk] = v
                        ins = o["emit"](eng)
                        if o["lane"] is not None:
                            ins.then_inc(semof(("d", e, o["lane"])), 16)
                        elif need[i]:
                            ins.then_inc(esem[e], 1)

                getattr(block, e)(body)


class Ctx:
    def __init__(self, nc):
        self.nc = nc
        self.p = Prog(nc)
        self.scopes = []

    def push_scope(self):
        self.scopes.append(([], len(self.p.ops)))

    def pop_scope(self):
        guards, start = self.scopes.pop()
        for g in reversed(guards):
            g.__exit__(None, None, None)
        self.p.range_fence(start)

    def salloc(self, name, shape, dtype, **kw):
        g = self.nc.sbuf_tensor(name, shape, dtype, **kw)
        t = g.__enter__()
        if self.scopes:
            self.scopes[-1][0].append(g)
        return t


def make_consts(nc):
    c = {}
    c["ident_bf"] = nc.inline_tensor(np.eye(128, dtype=bf), name="c_ident_bf")
    c["ident_f32"] = nc.inline_tensor(np.eye(128, dtype=np.float32), name="c_ident_f32")
    iu = np.triu(np.ones((128, 128), dtype=np.float32), k=0).astype(bf)
    c["triu_incl"] = nc.inline_tensor(iu, name="c_triu")
    for j in range(4):
        m = np.zeros((128, 512), dtype=np.float32)
        for k in range(128):
            m[k, j * 128 + k:] = 1.0
        c[f"tri01_{j}"] = nc.inline_tensor(m.astype(bf), name=f"c_tri01_{j}")
    c["iota16"] = nc.inline_tensor(
        np.tile(np.arange(16, dtype=np.float32), (128, 1)), name="c_iota16")
    c["ones_col_f32"] = nc.inline_tensor(np.ones((128, 1), dtype=np.float32),
                                         name="c_ones_col_f32")
    c["ones_1x128_f32"] = nc.inline_tensor(np.ones((1, 128), dtype=np.float32),
                                           name="c_ones_1x128_f32")
    su = np.triu(np.ones((8, 8), dtype=np.float32), k=1)
    c["striu8"] = nc.inline_tensor(su, name="c_striu8")
    c["ones_col_bf"] = nc.inline_tensor(np.ones((128, 1), dtype=bf), name="c_ones_col_bf")
    return c


def load_consts(ctx, names):
    nc, p = ctx.nc, ctx.p
    ctx.c = {}
    for nm in names:
        h = ctx.c_dram[nm]
        t = ctx.salloc(f"sb_{nm}", list(h.shape), h.dtype)
        ctx.c[nm] = t
        p.op("sync", lambda e, t=t, h=h: e.dma_start(out=t[:, :], in_=h.ap()),
             writes=[("c", nm)], dma=True, tag=f"ld_{nm}")


def emit_x_to_fm(ctx, x_in, xfm, key):
    nc, p = ctx.nc, ctx.p
    PS = ctx.PS
    ctx.push_scope()
    xt = ctx.salloc(f"x2f_in_{key}", [128, 1024], F32)
    ot = ctx.salloc(f"x2f_out_{key}", [128, 128], F32)
    for tt in range(T // 128):
        p.op("sync", lambda e, tt=tt: e.dma_start(
            out=xt[:, :], in_=x_in[tt * 128:(tt + 1) * 128, :]),
            writes=[("x2f_in", key)], dma=True)
        for cdm in range(8):
            bank = cdm % 2 + 4
            p.op("tensor", lambda e, cdm=cdm, bank=bank: e.transpose(
                PS[0:128, bank * 512:bank * 512 + 128], xt[:, cdm * 128:(cdm + 1) * 128],
                ctx.c["ident_f32"][:, :]),
                reads=[("x2f_in", key), ("c", "ident_f32")], writes=[("ps", bank)])
            p.op("vector", lambda e, bank=bank: e.tensor_copy(
                ot[:, :], PS[0:128, bank * 512:bank * 512 + 128]),
                reads=[("ps", bank)], writes=[("x2f_out", key)])
            p.op("sync", lambda e, tt=tt, cdm=cdm: e.dma_start(
                out=xfm[cdm * 128:(cdm + 1) * 128, tt * 128:(tt + 1) * 128], in_=ot[:, :]),
                reads=[("x2f_out", key)], writes=[("xfm", key, cdm, tt % 8)], dma=True)
    ctx.pop_scope()


def emit_fm_to_tok_f32(ctx, src_fm, dst_tok, key):
    nc, p = ctx.nc, ctx.p
    PS = ctx.PS
    ctx.push_scope()
    it = ctx.salloc(f"f2t_in_{key}", [128, T], F32)
    ot = ctx.salloc(f"f2t_out_{key}", [128, 128], F32)
    for cdm in range(8):
        p.op("sync", lambda e, cdm=cdm: e.dma_start(
            out=it[:, :], in_=src_fm[cdm * 128:(cdm + 1) * 128, :]),
            reads=[("f2t-src", key, cdm)], writes=[("f2t_in", key)], dma=True)
        for tt in range(T // 128):
            bank = tt % 2 + 4
            p.op("tensor", lambda e, tt=tt, bank=bank: e.transpose(
                PS[0:128, bank * 512:bank * 512 + 128], it[:, tt * 128:(tt + 1) * 128],
                ctx.c["ident_f32"][:, :]),
                reads=[("f2t_in", key), ("c", "ident_f32")], writes=[("ps", bank)])
            p.op("vector", lambda e, bank=bank: e.tensor_copy(
                ot[:, :], PS[0:128, bank * 512:bank * 512 + 128]),
                reads=[("ps", bank)], writes=[("f2t_out", key)])
            p.op("sync", lambda e, tt=tt, cdm=cdm: e.dma_start(
                out=dst_tok[tt * 128:(tt + 1) * 128, cdm * 128:(cdm + 1) * 128], in_=ot[:, :]),
                reads=[("f2t_out", key)], writes=[("f2t-out", key, tt % 8, cdm)], dma=True)
    ctx.pop_scope()


def emit_ln(ctx, xfm, xnfm, key, gate_cb=None):
    nc, p = ctx.nc, ctx.p
    PS = ctx.PS
    ctx.push_scope()
    xs = [ctx.salloc(f"ln_x_{key}_{c}", [128, TS], F32) for c in range(8)]
    xsq = ctx.salloc(f"ln_xsq_{key}", [128, TS], F32)
    rowbuf = ctx.salloc(f"ln_rows_{key}", [1, 6 * TS], F32)
    mu = rowbuf[0:1, 0:TS]
    msq = rowbuf[0:1, TS:2 * TS]
    mu2 = rowbuf[0:1, 2 * TS:3 * TS]
    var = rowbuf[0:1, 3 * TS:4 * TS]
    sd = rowbuf[0:1, 4 * TS:5 * TS]
    rstd = rowbuf[0:1, 5 * TS:6 * TS]
    Mb = ctx.salloc(f"ln_Mb_{key}", [128, TS], F32)
    Rb = ctx.salloc(f"ln_Rb_{key}", [128, TS], F32)
    tmp = ctx.salloc(f"ln_tmp_{key}", [128, TS], F32)
    xnb = ctx.salloc(f"ln_xnb_{key}", [128, TS], BF16)
    eps = ctx.salloc(f"ln_eps_{key}", [1, 1], F32)
    p.op("vector", lambda e: e.memset(eps[:, :], 1e-5), writes=[("lneps", key)])
    oc = ctx.c["ones_col_f32"]
    o1x = ctx.c["ones_1x128_f32"]

    for ts in range(NTS):
        for c in range(8):
            p.op("sync", lambda e, c=c, ts=ts: e.dma_start(
                out=xs[c][:, :], in_=xfm[c * 128:(c + 1) * 128, ts * TS:(ts + 1) * TS]),
                reads=[("xfm-src", key, c, ts)], writes=[("lnx", key, c)], dma=True)
        for c in range(8):
            p.op("tensor", lambda e, c=c: e.matmul(
                PS[0:1, 0:TS], oc[:, :], xs[c][:, :], start=(c == 0), stop=(c == 7)),
                reads=[("lnx", key, c), ("c", "ones_col_f32")], writes=[("ps", 0)])
            p.op("scalar", lambda e, c=c: e.activation(xsq[:, :], xs[c][:, :], AF.Square),
                 reads=[("lnx", key, c)], writes=[("lnxsq", key)])
            p.op("tensor", lambda e, c=c: e.matmul(
                PS[0:1, 512:512 + TS], oc[:, :], xsq[:, :], start=(c == 0), stop=(c == 7)),
                reads=[("lnxsq", key), ("c", "ones_col_f32")], writes=[("ps", 1)])
        p.op("vector", lambda e: e.tensor_scalar_mul(mu, PS[0:1, 0:TS], 1.0 / D),
             reads=[("ps", 0)], writes=[("lnrow", key, "mu")])
        p.op("vector", lambda e: e.tensor_scalar_mul(msq, PS[0:1, 512:512 + TS], 1.0 / D),
             reads=[("ps", 1)], writes=[("lnrow", key, "msq")])
        p.op("scalar", lambda e: e.activation(mu2, mu, AF.Square),
             reads=[("lnrow", key, "mu")], writes=[("lnrow", key, "mu2")])
        p.op("vector", lambda e: e.tensor_sub(var, msq, mu2),
             reads=[("lnrow", key, "msq"), ("lnrow", key, "mu2")],
             writes=[("lnrow", key, "var")])
        p.op("scalar", lambda e: e.activation(sd, var, AF.Sqrt, bias=eps[0:1, 0:1]),
             reads=[("lnrow", key, "var"), ("lneps", key)], writes=[("lnrow", key, "sd")])
        p.op("vector", lambda e: e.reciprocal(rstd, sd),
             reads=[("lnrow", key, "sd")], writes=[("lnrow", key, "rstd")])
        p.op("tensor", lambda e: e.matmul(PS[0:128, 1024:1024 + TS], o1x[:, :], mu,
                                          start=True, stop=True),
             reads=[("lnrow", key, "mu"), ("c", "ones_1x128_f32")], writes=[("ps", 2)])
        p.op("vector", lambda e: e.tensor_copy(Mb[:, :], PS[0:128, 1024:1024 + TS]),
             reads=[("ps", 2)], writes=[("lnMb", key)])
        p.op("tensor", lambda e: e.matmul(PS[0:128, 1536:1536 + TS], o1x[:, :], rstd,
                                          start=True, stop=True),
             reads=[("lnrow", key, "rstd"), ("c", "ones_1x128_f32")], writes=[("ps", 3)])
        p.op("scalar", lambda e: e.copy(Rb[:, :], PS[0:128, 1536:1536 + TS]),
             reads=[("ps", 3)], writes=[("lnRb", key)])
        for c in range(8):
            p.op("vector", lambda e, c=c: e.tensor_sub(tmp[:, :], xs[c][:, :], Mb[:, :]),
                 reads=[("lnx", key, c), ("lnMb", key)], writes=[("lntmp", key)])
            p.op("vector", lambda e: e.tensor_mul(xnb[:, :], tmp[:, :], Rb[:, :]),
                 reads=[("lntmp", key), ("lnRb", key)], writes=[("lnxnb", key)])
            p.op("sync", lambda e, c=c, ts=ts: e.dma_start(
                out=xnfm[c * 128:(c + 1) * 128, ts * TS:(ts + 1) * TS], in_=xnb[:, :]),
                reads=[("lnxnb", key)], writes=[("xnfm", key, c, ts)], dma=True)
        if gate_cb is not None:
            gate_cb(ts, mu, rstd, Rb, xs)
    ctx.pop_scope()


def emit_qkv(ctx, xnfm, qfm, kfm, wq_src, wk_src, wv_src, vptiles, key):
    nc, p = ctx.nc, ctx.p
    PS = ctx.PS
    ctx.push_scope()
    wqt = [ctx.salloc(f"qk_wq_{key}_{c}", [128, 1024], BF16, is_transpose_dest=True) for c in range(8)]
    wkt = [ctx.salloc(f"qk_wk_{key}_{c}", [128, 1024], BF16, is_transpose_dest=True) for c in range(8)]
    wvt = [ctx.salloc(f"qk_wv_{key}_{c}", [128, 1024], BF16, is_transpose_dest=True) for c in range(8)]
    for c in range(8):
        wq_src(c, wqt[c], ("qkvw", key, "q", c))
        wk_src(c, wkt[c], ("qkvw", key, "k", c))
        wv_src(c, wvt[c], ("qkvw", key, "v", c))
    xns = [ctx.salloc(f"qk_xn_{key}_{c}", [128, TS], BF16) for c in range(8)]
    ev = ctx.salloc(f"qk_ev_{key}", [128, 2 * TS], BF16)
    for tt in range(T // 128):
        p.op("vector", lambda e, tt=tt: e.memset(vptiles[tt][:, :], 1.0),
             writes=[("vp", tt)])
    for ts in range(NTS):
        for c in range(8):
            p.op("sync", lambda e, c=c, ts=ts: e.dma_start(
                out=xns[c][:, :], in_=xnfm[c * 128:(c + 1) * 128, ts * TS:(ts + 1) * TS]),
                reads=[("xnfm-src", key, c, ts)], writes=[("qkvxn", key, c)], dma=True)
        for wt, dstfm, nm in [(wqt, qfm, "q"), (wkt, kfm, "k")]:
            for hp in range(8):
                bank = hp % 2
                for c in range(8):
                    p.op("tensor", lambda e, wt=wt, hp=hp, c=c, bank=bank: e.matmul(
                        PS[0:128, bank * 512:bank * 512 + TS],
                        wt[c][:, hp * 128:(hp + 1) * 128], xns[c][:, :],
                        start=(c == 0), stop=(c == 7)),
                        reads=[("qkvw", key, nm, c), ("qkvxn", key, c)],
                        writes=[("ps", bank)])
                half = hp % 2
                p.op("vector", lambda e, half=half, bank=bank: e.tensor_copy(
                    ev[:, half * TS:(half + 1) * TS], PS[0:128, bank * 512:bank * 512 + TS]),
                    reads=[("ps", bank)], writes=[("qkvev", key, half)])
                p.op("sync", lambda e, hp=hp, ts=ts, dstfm=dstfm, half=half, nm=nm: e.dma_start(
                    out=dstfm[hp * 128:(hp + 1) * 128, ts * TS:(ts + 1) * TS],
                    in_=ev[:, half * TS:(half + 1) * TS]),
                    reads=[("qkvev", key, half)], writes=[(nm + "fm", key, hp, ts)], dma=True)
        for sub in range(TS // 128):
            tt = ts * (TS // 128) + sub
            bank = 2 + (sub % 2) * 2
            for half in range(2):
                for c in range(8):
                    p.op("tensor", lambda e, c=c, bank=bank, sub=sub, half=half: e.matmul(
                        PS[0:128, (bank + half) * 512:(bank + half) * 512 + 512],
                        xns[c][:, sub * 128:(sub + 1) * 128],
                        wvt[c][:, half * 512:(half + 1) * 512],
                        start=(c == 0), stop=(c == 7)),
                        reads=[("qkvxn", key, c), ("qkvw", key, "v", c)],
                        writes=[("ps", bank + half)])
            p.op("vector", lambda e, tt=tt, bank=bank: e.tensor_copy(
                vptiles[tt][:, :].rearrange("p (h v) -> p h v", h=H, v=65)[:, :, 0:64],
                PS[0:128, bank * 512:bank * 512 + 1024].rearrange("p (h v) -> p h v", h=H, v=64)),
                reads=[("ps", bank), ("ps", bank + 1)], writes=[("vp", tt)])
    ctx.pop_scope()


def emit_flash(ctx, qfm, kfm, vptiles, avfm, causal, key):
    nc, p = ctx.nc, ctx.p
    PS = ctx.PS
    ctx.push_scope()
    kh = [ctx.salloc(f"fl_k_{key}_{i}", [64, T], BF16) for i in range(2)]
    qh = [ctx.salloc(f"fl_q_{key}_{i}", [64, T], BF16) for i in range(2)]
    eb = [ctx.salloc(f"fl_e_{key}_{i}", [128, 2 * TS], BF16) for i in range(2)]
    em = [ctx.salloc(f"fl_em_{key}_{i}", [128, 2 * TS], BF16) for i in range(2)]
    recl = ctx.salloc(f"fl_recl_{key}", [1, TS], F32)
    recb = ctx.salloc(f"fl_recb_{key}", [1, TS], BF16)
    bcast = ctx.salloc(f"fl_bc_{key}", [64, TS], F32)
    avh = [ctx.salloc(f"fl_avh_{key}_{i}", [64, TS], BF16) for i in range(2)]
    ones64 = ctx.salloc(f"fl_o64_{key}", [1, 64], BF16)
    bias20 = ctx.salloc(f"fl_b20_{key}", [128, 1], F32)
    p.op("vector", lambda e: e.memset(ones64[:, :], 1.0), writes=[("flo64", key)])
    p.op("vector", lambda e: e.memset(bias20[:, :], -20.0), writes=[("flb20", key)])

    def emit_scores_pair(h, hb, qs, kt0, kt1, pb):
        for u, kt in enumerate((kt0, kt1)):
            bank = 2 * pb + u
            p.op("tensor", lambda e, kt=kt, qs=qs, bank=bank, hb=hb: e.matmul(
                PS[0:128, bank * 512:bank * 512 + TS],
                kh[hb][:, kt * 128:(kt + 1) * 128], qh[hb][:, qs * TS:(qs + 1) * TS],
                start=True, stop=True),
                reads=[("flk", key, hb), ("flq", key, hb)], writes=[("ps", bank)])
        p.op("scalar", lambda e, pb=pb: e.activation(
            eb[pb][:, :], PS[0:128, 2 * pb * 512:2 * pb * 512 + 2 * TS], AF.Exp,
            bias=bias20[:, 0:1], scale=0.125),
            reads=[("ps", 2 * pb), ("ps", 2 * pb + 1), ("flb20", key)],
            writes=[("fle", key, pb)])
        for u, kt in enumerate((kt0, kt1)):
            if causal and kt >= qs * 4:
                jj = kt - qs * 4
                p.op("vector", lambda e, pb=pb, u=u, jj=jj: e.tensor_mul(
                    em[pb][:, u * TS:(u + 1) * TS], eb[pb][:, u * TS:(u + 1) * TS],
                    ctx.c[f"tri01_{jj}"][:, :]),
                    reads=[("fle", key, pb), ("c", f"tri01_{jj}")],
                    writes=[("flem", key, pb, u)])

    def emit_av_pair(h, hb, qs, kt0, kt1, pb, j0, nkt, avbank):
        for u, kt in enumerate((kt0, kt1)):
            if causal and kt >= qs * 4:
                esrc = em[pb][:, u * TS:(u + 1) * TS]
                ekey = ("flem", key, pb, u)
            else:
                esrc = eb[pb][:, u * TS:(u + 1) * TS]
                ekey = ("fle", key, pb)
            j = j0 + u
            p.op("tensor", lambda e, kt=kt, esrc=esrc, avbank=avbank, h=h, j=j, nkt=nkt: e.matmul(
                PS[0:65, avbank * 512:avbank * 512 + TS],
                vptiles[kt][:, h * 65:(h + 1) * 65], esrc,
                start=(j == 0), stop=(j == nkt - 1)),
                reads=[("vp", kt), ekey], writes=[("ps", avbank)])

    def mk_norm(h, qs, avbank, ab):
        def reciprocal():
            p.op("vector", lambda e, avbank=avbank: e.reciprocal(
                recl[:, :], PS[64:65, avbank * 512:avbank * 512 + TS]),
                reads=[("ps", avbank)], writes=[("flrecl", key)])
            p.op("vector", lambda e: e.tensor_copy(recb[:, :], recl[:, :]),
                 reads=[("flrecl", key)], writes=[("flrecb", key)])

        def bcast_mm():
            p.op("tensor", lambda e: e.matmul(
                PS[0:64, 6 * 512:6 * 512 + TS], ones64[:, :], recb[:, :],
                start=True, stop=True),
                reads=[("flrecb", key), ("flo64", key)], writes=[("ps", 6)])

        def tail():
            p.op("scalar", lambda e: e.copy(bcast[:, :], PS[0:64, 6 * 512:6 * 512 + TS]),
                 reads=[("ps", 6)], writes=[("flbc", key)])
            p.op("vector", lambda e, avbank=avbank, ab=ab: e.tensor_mul(
                avh[ab][:, :], PS[0:64, avbank * 512:avbank * 512 + TS], bcast[:, :]),
                reads=[("ps", avbank), ("flbc", key)], writes=[("flavh", key, ab)])
            p.op("sync", lambda e, h=h, qs=qs, ab=ab: e.dma_start(
                out=avfm[h * 64:(h + 1) * 64, qs * TS:(qs + 1) * TS], in_=avh[ab][:, :]),
                reads=[("flavh", key, ab)], writes=[("avfm", key, h, qs)], dma=True)

        return (reciprocal, bcast_mm, tail)

    pending = None
    for h in range(H):
        hb = h % 2
        p.op("sync", lambda e, h=h, hb=hb: e.dma_start(
            out=kh[hb][:, :], in_=kfm[h * 64:(h + 1) * 64, :]),
            reads=[("kfm-all", key)], writes=[("flk", key, hb)], dma=True)
        p.op("sync", lambda e, h=h, hb=hb: e.dma_start(
            out=qh[hb][:, :], in_=qfm[h * 64:(h + 1) * 64, :]),
            reads=[("qfm-all", key)], writes=[("flq", key, hb)], dma=True)
        for qs in range(NTS):
            avbank = 4 + qs % 2
            ab = qs % 2
            kts = list(range(0, (qs * 4 + 4) if causal else 32))
            nkt = len(kts)
            pairs = [(kts[i], kts[i + 1]) for i in range(0, nkt, 2)]
            npair = len(pairs)
            emit_scores_pair(h, hb, qs, *pairs[0], 0)
            if pending is not None:
                pending[0]()
            for pi in range(1, npair):
                emit_scores_pair(h, hb, qs, *pairs[pi], pi % 2)
                if pending is not None and pi == 1:
                    pending[1]()
                    pending[2]()
                    pending = None
                emit_av_pair(h, hb, qs, *pairs[pi - 1], (pi - 1) % 2,
                             2 * (pi - 1), nkt, avbank)
            emit_av_pair(h, hb, qs, *pairs[npair - 1], (npair - 1) % 2,
                         2 * (npair - 1), nkt, avbank)
            if pending is not None:
                pending[1]()
                pending[2]()
            pending = mk_norm(h, qs, avbank, ab)
    pending[0]()
    pending[1]()
    pending[2]()
    ctx.pop_scope()


def emit_quantize(ctx, w_in, rows, cols, wq_dram, key, wqT_dram=None, scoped=True):
    nc, p = ctx.nc, ctx.p
    PS = ctx.PS
    if scoped:
        ctx.push_scope()
    CB = 1024
    ncb = cols // CB
    nt = rows // 128
    wt = ctx.salloc(f"qz_w_{key}", [128, CB], F32)
    at = ctx.salloc(f"qz_a_{key}", [128, CB], F32)
    acc = ctx.salloc(f"qz_acc_{key}", [128, nt * ncb], F32)
    accs = ctx.salloc(f"qz_accs_{key}", [128, 1], F32)
    gam = ctx.salloc(f"qz_gam_{key}", [1, 1], F32)
    invg = ctx.salloc(f"qz_invg_{key}", [1, 1], F32)
    invgb = ctx.salloc(f"qz_invgb_{key}", [128, 1], F32)
    z = ctx.salloc(f"qz_z_{key}", [128, CB], F32)
    a01 = ctx.salloc(f"qz_a01_{key}", [128, CB], F32)
    b01 = ctx.salloc(f"qz_b01_{key}", [128, CB], F32)
    q = ctx.salloc(f"qz_q_{key}", [128, CB], BF16)
    if wqT_dram is not None:
        qf = ctx.salloc(f"qz_qf_{key}", [128, CB], F32)
        qtb = ctx.salloc(f"qz_qtb_{key}", [128, 128], BF16)
    oc = ctx.c["ones_col_f32"]
    o1x = ctx.c["ones_1x128_f32"]
    k = 0
    for t in range(nt):
        for cb in range(ncb):
            p.op("sync", lambda e, t=t, cb=cb: e.dma_start(
                out=wt[:, :], in_=w_in[t * 128:(t + 1) * 128, cb * CB:(cb + 1) * CB]),
                writes=[("qzw", key)], dma=True)
            p.op("scalar", lambda e: e.activation(at[:, :], wt[:, :], AF.Abs),
                 reads=[("qzw", key)], writes=[("qza", key)])
            p.op("vector", lambda e, k=k: e.reduce_sum(acc[:, k:k + 1], at[:, :], axis=AX.X),
                 reads=[("qza", key)], writes=[("qzacc", key, k)])
            k += 1
    p.op("vector", lambda e: e.reduce_sum(accs[:, :], acc[:, :], axis=AX.X),
         reads=[("qzacc", key, k2) for k2 in range(nt * ncb)], writes=[("qzaccs", key)])
    p.op("tensor", lambda e: e.matmul(PS[0:1, 6 * 512:6 * 512 + 1], accs[:, :], oc[:, 0:1],
                                      start=True, stop=True),
         reads=[("qzaccs", key), ("c", "ones_col_f32")], writes=[("ps", 6)])
    p.op("vector", lambda e: e.tensor_scalar(gam[:, :], PS[0:1, 6 * 512:6 * 512 + 1],
                                             1.0 / (rows * cols), 1e-5, ALU.mult, ALU.add),
         reads=[("ps", 6)], writes=[("qzgam", key)])
    p.op("vector", lambda e: e.reciprocal(invg[:, :], gam[:, :]),
         reads=[("qzgam", key)], writes=[("qzinvg", key)])
    p.op("tensor", lambda e: e.matmul(PS[0:128, 7 * 512:7 * 512 + 1], o1x[:, :], invg[:, :],
                                      start=True, stop=True),
         reads=[("qzinvg", key), ("c", "ones_1x128_f32")], writes=[("ps", 7)])
    p.op("vector", lambda e: e.tensor_copy(invgb[:, :], PS[0:128, 7 * 512:7 * 512 + 1]),
         reads=[("ps", 7)], writes=[("qzinvgb", key)])
    for t in range(nt):
        for cb in range(ncb):
            p.op("sync", lambda e, t=t, cb=cb: e.dma_start(
                out=wt[:, :], in_=w_in[t * 128:(t + 1) * 128, cb * CB:(cb + 1) * CB]),
                writes=[("qzw", key)], dma=True)
            p.op("scalar", lambda e: e.activation(z[:, :], wt[:, :], AF.Copy,
                                                  scale=invgb[:, 0:1]),
                 reads=[("qzw", key), ("qzinvgb", key)], writes=[("qzz", key)])
            p.op("vector", lambda e: e.tensor_scalar(a01[:, :], z[:, :], 0.5, None, ALU.is_gt),
                 reads=[("qzz", key)], writes=[("qza01", key)])
            p.op("vector", lambda e: e.tensor_scalar(b01[:, :], z[:, :], -0.5, None, ALU.is_lt),
                 reads=[("qzz", key)], writes=[("qzb01", key)])
            p.op("vector", lambda e: e.tensor_sub(q[:, :], a01[:, :], b01[:, :]),
                 reads=[("qza01", key), ("qzb01", key)], writes=[("qzq", key)])
            p.op("sync", lambda e, t=t, cb=cb: e.dma_start(
                out=wq_dram[t * 128:(t + 1) * 128, cb * CB:(cb + 1) * CB], in_=q[:, :]),
                reads=[("qzq", key)], writes=[("qdram", key, t)], dma=True)
            if wqT_dram is not None:
                p.op("vector", lambda e: e.tensor_sub(qf[:, :], a01[:, :], b01[:, :]),
                     reads=[("qza01", key), ("qzb01", key)], writes=[("qzqf", key)])
                for b in range(CB // 128):
                    p.op("tensor", lambda e, b=b: e.transpose(
                        PS[0:128, 6 * 512:6 * 512 + 128], qf[:, b * 128:(b + 1) * 128],
                        ctx.c["ident_f32"][:, :]),
                        reads=[("qzqf", key), ("c", "ident_f32")], writes=[("ps", 6)])
                    p.op("vector", lambda e, b=b: e.tensor_copy(
                        qtb[:, :], PS[0:128, 6 * 512:6 * 512 + 128]),
                        reads=[("ps", 6)], writes=[("qzqtb", key)])
                    p.op("sync", lambda e, t=t, cb=cb, b=b: e.dma_start(
                        out=wqT_dram[cb * CB + b * 128:cb * CB + (b + 1) * 128,
                                     t * 128:(t + 1) * 128], in_=qtb[:, :]),
                        reads=[("qzqtb", key)], writes=[("qtdram", key, t, cb, b)], dma=True)
    if scoped:
        ctx.pop_scope()


def emit_epilogue_l0(ctx, xfm, xnfm, avfm, wob, w1qb, w2qtb, b1_in, b2_in, x1fm, key):
    nc, p = ctx.nc, ctx.p
    PS = ctx.PS
    ctx.push_scope()
    wot = [ctx.salloc(f"ep_wot_{key}_{h}", [64, 1024], BF16) for h in range(H)]
    wtmp = ctx.salloc(f"ep_wtmp_{key}", [128, 64], F32)
    for h in range(H):
        for c in range(8):
            p.op("sync", lambda e, h=h, c=c: e.dma_start(
                out=wtmp[:, :], in_=wob[h * 1024 + c * 128: h * 1024 + (c + 1) * 128, :]),
                writes=[("epwtmp", key)], dma=True)
            p.op("tensor", lambda e: e.transpose(PS[0:64, 7 * 512:7 * 512 + 128], wtmp[:, :],
                                                 ctx.c["ident_f32"][:, :]),
                 reads=[("epwtmp", key), ("c", "ident_f32")], writes=[("ps", 7)])
            p.op("vector", lambda e, h=h, c=c: e.tensor_copy(
                wot[h][:, c * 128:(c + 1) * 128], PS[0:64, 7 * 512:7 * 512 + 128]),
                reads=[("ps", 7)], writes=[("epwot", key, h)])
    b1s = ctx.salloc(f"ep_b1s_{key}", [32, 128], F32)
    b1t = ctx.salloc(f"ep_b1t_{key}", [128, 32], F32)
    p.op("sync", lambda e: e.dma_start(out=b1s[:, :], in_=b1_in),
         writes=[("epb1s", key)], dma=True)
    p.op("tensor", lambda e: e.transpose(PS[0:128, 5 * 512:5 * 512 + 32], b1s[:, :],
                                         ctx.c["ident_f32"][0:32, 0:32]),
         reads=[("epb1s", key), ("c", "ident_f32")], writes=[("ps", 5)])
    p.op("vector", lambda e: e.tensor_copy(b1t[:, :], PS[0:128, 5 * 512:5 * 512 + 32]),
         reads=[("ps", 5)], writes=[("epb1t", key)])
    b2row = ctx.salloc(f"ep_b2_{key}", [1, 1024], BF16)
    p.op("gpsimd", lambda e: e.dma_start(out=b2row[:, :], in_=b2_in),
         writes=[("epb2", key)], dma=True)
    onesr = ctx.salloc(f"ep_ones_{key}", [1, TS], BF16)
    p.op("vector", lambda e: e.memset(onesr[:, :], 1.0), writes=[("epones", key)])
    w1qt = [ctx.salloc(f"ep_w1qt_{key}_{c}", [128, FF], BF16, is_transpose_dest=True)
            for c in range(8)]
    for c in range(8):
        p.op("sync", lambda e, c=c: e.dma_start_transpose(
            w1qt[c][:, :], w1qb[:, c * 128:(c + 1) * 128]),
            reads=[("qdram", ("l0w1",), t) for t in range(32)],
            writes=[("epw1qt", key, c)], dma=True)
    w2qt = [ctx.salloc(f"ep_w2qt_{key}_{i}", [128, 1024], BF16) for i in range(8)]
    xns = [ctx.salloc(f"ep_xn_{key}_{c}", [128, TS], BF16) for c in range(8)]
    hsl = [ctx.salloc(f"ep_h_{key}_{f}", [128, TS], BF16) for f in range(32)]
    avt = [ctx.salloc(f"ep_av_{key}_{i}", [64, TS], BF16) for i in range(4)]
    xres = [ctx.salloc(f"ep_xres_{key}_{i}", [128, TS], F32) for i in range(2)]
    osum = [ctx.salloc(f"ep_osum_{key}_{i}", [128, TS], F32) for i in range(2)]

    for ts in range(NTS):
        for c in range(8):
            p.op("sync", lambda e, c=c, ts=ts: e.dma_start(
                out=xns[c][:, :], in_=xnfm[c * 128:(c + 1) * 128, ts * TS:(ts + 1) * TS]),
                reads=[("xnfm-src2", key, c, ts)], writes=[("epxn", key, c)], dma=True)
        for fc in range(32):
            bank = 4 + fc % 2
            for c in range(8):
                p.op("tensor", lambda e, fc=fc, c=c, bank=bank: e.matmul(
                    PS[0:128, bank * 512:(bank) * 512 + TS],
                    w1qt[c][:, fc * 128:(fc + 1) * 128], xns[c][:, :],
                    start=(c == 0), stop=(c == 7)),
                    reads=[("epw1qt", key, c), ("epxn", key, c)], writes=[("ps", bank)])
            p.op("scalar", lambda e, fc=fc, bank=bank: e.activation(
                hsl[fc][:, :], PS[0:128, bank * 512:bank * 512 + TS], AF.Relu,
                bias=b1t[:, fc:fc + 1]),
                reads=[("ps", bank), ("epb1t", key)], writes=[("eph", key, fc)])
        for h in range(H):
            hb = h % 4
            p.op("sync", lambda e, h=h, ts=ts, hb=hb: e.dma_start(
                out=avt[hb][:, :], in_=avfm[h * 64:(h + 1) * 64, ts * TS:(ts + 1) * TS]),
                reads=[("avfm", key, h, ts)], writes=[("epav", key, hb)], dma=True)
            for ds in range(8):
                p.op("tensor", lambda e, h=h, ds=ds, hb=hb: e.matmul(
                    PS[0:128, ds * 512:ds * 512 + TS],
                    wot[h][:, ds * 128:(ds + 1) * 128], avt[hb][:, :],
                    start=(h == 0), stop=False),
                    reads=[("epwot", key, h), ("epav", key, hb)], writes=[("ps", ds)])
        for fc in range(32):
            wi = fc % 8
            p.op("sync", lambda e, fc=fc, wi=wi: e.dma_start(
                out=w2qt[wi][:, :], in_=w2qtb[fc * 128:(fc + 1) * 128, :]),
                reads=[("qtdram", ("l0w2",), t2, cb2, b2_) for t2 in range(8)
                       for cb2 in range(4) for b2_ in range(8)],
                writes=[("epw2buf", key, wi)], dma=True)
            for ds in range(8):
                p.op("tensor", lambda e, fc=fc, ds=ds, wi=wi: e.matmul(
                    PS[0:128, ds * 512:ds * 512 + TS],
                    w2qt[wi][:, ds * 128:(ds + 1) * 128], hsl[fc][:, :],
                    start=False, stop=False),
                    reads=[("epw2buf", key, wi), ("eph", key, fc)], writes=[("ps", ds)])
        for ds in range(8):
            db = ds % 2
            p.op("tensor", lambda e, ds=ds: e.matmul(
                PS[0:128, ds * 512:ds * 512 + TS],
                b2row[0:1, ds * 128:(ds + 1) * 128], onesr[:, :],
                start=False, stop=True),
                reads=[("epb2", key), ("epones", key)], writes=[("ps", ds)])
            p.op("sync", lambda e, ds=ds, ts=ts, db=db: e.dma_start(
                out=xres[db][:, :], in_=xfm[ds * 128:(ds + 1) * 128, ts * TS:(ts + 1) * TS]),
                reads=[("xfm-src3", key, ds, ts)], writes=[("epxres", key, db)], dma=True)
            p.op("vector", lambda e, ds=ds, db=db: e.tensor_add(
                osum[db][:, :], PS[0:128, ds * 512:ds * 512 + TS], xres[db][:, :]),
                reads=[("ps", ds), ("epxres", key, db)], writes=[("eposum", key, db)])
            p.op("sync", lambda e, ds=ds, ts=ts, db=db: e.dma_start(
                out=x1fm[ds * 128:(ds + 1) * 128, ts * TS:(ts + 1) * TS], in_=osum[db][:, :]),
                reads=[("eposum", key, db)], writes=[("x1fm", key, ds, ts)], dma=True)
    ctx.pop_scope()


def emit_gate_router(ctx, gwT, negs, gate_bufs, key):
    nc, p = ctx.nc, ctx.p
    PS = ctx.PS
    (gsl, gtok, mx, nmx, oh, cand, eidx, ohb, eg, seg, prob, rnk, rsel,
     offs, probsb, eidxsb, lranksb, routet) = gate_bufs
    o1x = ctx.c["ones_1x128_f32"]
    ocbf = ctx.c["ones_col_bf"]
    iot = ctx.c["iota16"]
    triu = ctx.c["triu_incl"]
    p.op("vector", lambda e: e.memset(offs[:, :], 0.0), writes=[("goffs", key)])

    def gate_cb(ts, mu, rstd, Rb, xs):
        for c in range(8):
            p.op("tensor", lambda e, c=c: e.matmul(
                PS[0:16, 6 * 512:6 * 512 + TS], gwT[c][:, :], xs[c][:, :],
                start=(c == 0), stop=False),
                reads=[("gwT", key, c), ("lnx", key, c)], writes=[("ps", 6)])
        p.op("tensor", lambda e: e.matmul(
            PS[0:16, 6 * 512:6 * 512 + TS], negs[:, :], mu, start=False, stop=True),
            reads=[("gnegs", key), ("lnrow", key, "mu")], writes=[("ps", 6)])
        p.op("vector", lambda e: e.tensor_mul(gsl[:, :], PS[0:16, 6 * 512:6 * 512 + TS],
                                              Rb[0:16, :]),
             reads=[("ps", 6), ("lnRb", key)], writes=[("ggsl", key)])
        for sub in range(TS // 128):
            tt = ts * (TS // 128) + sub
            p.op("tensor", lambda e, sub=sub: e.transpose(
                PS[0:128, 5 * 512:5 * 512 + 16], gsl[:, sub * 128:(sub + 1) * 128],
                ctx.c["ident_f32"][0:16, 0:16]),
                reads=[("ggsl", key), ("c", "ident_f32")], writes=[("ps", 5)])
            p.op("vector", lambda e: e.tensor_copy(gtok[:, :], PS[0:128, 5 * 512:5 * 512 + 16]),
                 reads=[("ps", 5)], writes=[("ggtok", key)])
            p.op("vector", lambda e: e.reduce_max(mx[:, :], gtok[:, :], axis=AX.X),
                 reads=[("ggtok", key)], writes=[("gmx", key)])
            p.op("vector", lambda e: e.tensor_scalar(oh[:, :], gtok[:, :], mx[:, 0:1],
                                                     None, ALU.is_ge),
                 reads=[("ggtok", key), ("gmx", key)], writes=[("goh", key)])
            p.op("vector", lambda e: e.tensor_scalar(cand[:, :], oh[:, :], -1e9, 1e9,
                                                     ALU.mult, ALU.add),
                 reads=[("goh", key)], writes=[("gcand", key)])
            p.op("vector", lambda e: e.tensor_add(cand[:, :], cand[:, :], iot[:, :]),
                 reads=[("gcand", key), ("c", "iota16")], writes=[("gcand", key)])
            p.op("vector", lambda e: e.tensor_reduce(eidx[:, :], cand[:, :], axis=AX.X,
                                                     op=ALU.min),
                 reads=[("gcand", key)], writes=[("geidx", key)])
            p.op("vector", lambda e: e.tensor_scalar(oh[:, :], iot[:, :], eidx[:, 0:1],
                                                     None, ALU.is_equal),
                 reads=[("c", "iota16"), ("geidx", key)], writes=[("goh", key)])
            p.op("vector", lambda e: e.tensor_copy(ohb[:, :], oh[:, :]),
                 reads=[("goh", key)], writes=[("gohb", key)])
            p.op("vector", lambda e: e.tensor_scalar_mul(nmx[:, :], mx[:, :], -1.0),
                 reads=[("gmx", key)], writes=[("gnmx", key)])
            p.op("scalar", lambda e: e.activation(eg[:, :], gtok[:, :], AF.Exp,
                                                  bias=nmx[:, 0:1]),
                 reads=[("ggtok", key), ("gnmx", key)], writes=[("geg", key)])
            p.op("vector", lambda e: e.reduce_sum(seg[:, :], eg[:, :], axis=AX.X),
                 reads=[("geg", key)], writes=[("gseg", key)])
            p.op("vector", lambda e, tt=tt: e.reciprocal(probsb[:, tt:tt + 1], seg[:, :]),
                 reads=[("gseg", key)], writes=[("gprob", key, tt)])
            p.op("tensor", lambda e: e.matmul(PS[0:128, 5 * 512:5 * 512 + 16],
                                              triu[:, :], ohb[:, :], start=True, stop=False),
                 reads=[("c", "triu_incl"), ("gohb", key)], writes=[("ps", 5)])
            p.op("tensor", lambda e: e.matmul(PS[0:128, 5 * 512:5 * 512 + 16],
                                              o1x[:, :], offs[:, :], start=False, stop=True),
                 reads=[("c", "ones_1x128_f32"), ("goffs", key)], writes=[("ps", 5)])
            p.op("vector", lambda e: e.tensor_mul(rnk[:, :], PS[0:128, 5 * 512:5 * 512 + 16],
                                                  oh[:, :]),
                 reads=[("ps", 5), ("goh", key)], writes=[("grnk", key)])
            p.op("vector", lambda e: e.reduce_sum(rsel[:, :], rnk[:, :], axis=AX.X),
                 reads=[("grnk", key)], writes=[("grsel", key)])
            p.op("vector", lambda e, tt=tt: e.tensor_scalar_add(lranksb[:, tt:tt + 1],
                                                                rsel[:, :], -1.0),
                 reads=[("grsel", key)], writes=[("glrank", key, tt)])
            p.op("vector", lambda e, tt=tt: e.tensor_copy(eidxsb[:, tt:tt + 1], eidx[:, :]),
                 reads=[("geidx", key)], writes=[("geidxsb", key, tt)])
            p.op("tensor", lambda e: e.matmul(PS[0:1, 6 * 512 + 0:6 * 512 + 16],
                                              ocbf[:, :], ohb[:, :], start=True, stop=True),
                 reads=[("c", "ones_col_bf"), ("gohb", key)], writes=[("ps", 6)])
            p.op("vector", lambda e: e.tensor_add(offs[:, :], offs[:, :],
                                                  PS[0:1, 6 * 512 + 0:6 * 512 + 16]),
                 reads=[("goffs", key), ("ps", 6)], writes=[("goffs", key)])

    return gate_cb


def emit_epilogue_l1(ctx, x1fm, avfm, wob_l1, out1pfm, key):
    nc, p = ctx.nc, ctx.p
    PS = ctx.PS
    ctx.push_scope()
    wo = [ctx.salloc(f"e1_wo_{key}_{c}", [128, 1024], BF16) for c in range(8)]
    for c in range(8):
        p.op("sync", lambda e, c=c: e.dma_start(
            out=wo[c][:, :], in_=wob_l1[c * 128:(c + 1) * 128, :]),
            reads=[("cast", "wo1")], writes=[("e1wo", key, c)], dma=True)
    avt = [ctx.salloc(f"e1_av_{key}_{i}", [128, TS], BF16) for i in range(2)]
    xres = [ctx.salloc(f"e1_xres_{key}_{i}", [128, TS], F32) for i in range(2)]
    osum = [ctx.salloc(f"e1_osum_{key}_{i}", [128, TS], F32) for i in range(2)]
    for ts in range(NTS):
        for c in range(8):
            cb = c % 2
            p.op("sync", lambda e, c=c, ts=ts, cb=cb: e.dma_start(
                out=avt[cb][:, :], in_=avfm[c * 128:(c + 1) * 128, ts * TS:(ts + 1) * TS]),
                reads=[("avfm", key, 2 * c, ts), ("avfm", key, 2 * c + 1, ts)],
                writes=[("e1av", key, cb)], dma=True)
            for ds in range(8):
                p.op("tensor", lambda e, c=c, ds=ds, cb=cb: e.matmul(
                    PS[0:128, ds * 512:ds * 512 + TS],
                    wo[c][:, ds * 128:(ds + 1) * 128], avt[cb][:, :],
                    start=(c == 0), stop=(c == 7)),
                    reads=[("e1wo", key, c), ("e1av", key, cb)], writes=[("ps", ds)])
        for ds in range(8):
            db = ds % 2
            p.op("sync", lambda e, ds=ds, ts=ts, db=db: e.dma_start(
                out=xres[db][:, :], in_=x1fm[ds * 128:(ds + 1) * 128, ts * TS:(ts + 1) * TS]),
                reads=[("x1-src", key, ds, ts)], writes=[("e1xres", key, db)], dma=True)
            p.op("vector", lambda e, ds=ds, db=db: e.tensor_add(
                osum[db][:, :], PS[0:128, ds * 512:ds * 512 + TS], xres[db][:, :]),
                reads=[("ps", ds), ("e1xres", key, db)], writes=[("e1osum", key, db)])
            p.op("sync", lambda e, ds=ds, ts=ts, db=db: e.dma_start(
                out=out1pfm[ds * 128:(ds + 1) * 128, ts * TS:(ts + 1) * TS],
                in_=osum[db][:, :]),
                reads=[("e1osum", key, db)], writes=[("out1p", key, ds, ts)], dma=True)
    ctx.pop_scope()


def emit_layer1(ctx, x1_in, wq_in, wk_in, wv_in, wo_in, gw_in, out1p,
                xn1aug_out, route_out, counts_out):
    nc, p = ctx.nc, ctx.p
    PS = ctx.PS
    xnfm = nc.dram_tensor("xnfm1", [D, T], BF16).ap()
    qfm = nc.dram_tensor("qfm1", [D, T], BF16).ap()
    kfm = nc.dram_tensor("kfm1", [D, T], BF16).ap()
    avfm = nc.dram_tensor("avfm1", [D, T], BF16).ap()
    wqb = nc.dram_tensor("wqb1", [H * D, DH], BF16).ap()
    wkb = nc.dram_tensor("wkb1", [H * D, DH], BF16).ap()
    wvb = nc.dram_tensor("wvb1", [H * D, DH], BF16).ap()
    wob = nc.dram_tensor("wob1", [D, D], BF16).ap()
    for wsrc, dst, nm in [(wq_in, wqb, "wq1"), (wk_in, wkb, "wk1"),
                          (wv_in, wvb, "wv1"), (wo_in, wob, "wo1")]:
        p.op("gpsimd", lambda e, wsrc=wsrc, dst=dst: e.dma_start(out=dst, in_=wsrc),
             writes=[("cast", nm)], dma=True)

    key = ("l1",)
    ctx.push_scope()
    gw_s = ctx.salloc("gw_s", [16, D], F32)
    p.op("sync", lambda e: e.dma_start(out=gw_s[:, :], in_=gw_in),
         writes=[("gws",)], dma=True)
    gwT = [ctx.salloc(f"gwT_{c}", [128, 16], F32) for c in range(8)]
    for c in range(8):
        p.op("tensor", lambda e, c=c: e.transpose(
            PS[0:128, 5 * 512:5 * 512 + 16], gw_s[:, c * 128:(c + 1) * 128],
            ctx.c["ident_f32"][0:16, 0:16]),
            reads=[("gws",), ("c", "ident_f32")], writes=[("ps", 5)])
        p.op("vector", lambda e, c=c: e.tensor_copy(gwT[c][:, :], PS[0:128, 5 * 512:5 * 512 + 16]),
             reads=[("ps", 5)], writes=[("gwT", key, c)])
    srow = ctx.salloc("srow", [16, 1], F32)
    negs = ctx.salloc("negs", [1, 16], F32)
    p.op("vector", lambda e: e.reduce_sum(srow[:, :], gw_s[:, :], axis=AX.X),
         reads=[("gws",)], writes=[("gsrow",)])
    p.op("vector", lambda e: e.tensor_scalar_mul(srow[:, :], srow[:, :], -1.0),
         reads=[("gsrow",)], writes=[("gsrow",)])
    id16 = ctx.salloc("id16", [16, 16], F32)
    p.op("sync", lambda e: e.dma_start(out=id16[:, :],
                                       in_=ctx.c_dram["ident_f32"].ap()[0:16, 0:16]),
         writes=[("gid16",)], dma=True)
    p.op("tensor", lambda e: e.matmul(PS[0:1, 5 * 512:5 * 512 + 16], srow[:, :],
                                      id16[:, :], start=True, stop=True),
         reads=[("gsrow",), ("gid16",)], writes=[("ps", 5)])
    p.op("vector", lambda e: e.tensor_copy(negs[:, :], PS[0:1, 5 * 512:5 * 512 + 16]),
         reads=[("ps", 5)], writes=[("gnegs", key)])

    gate_bufs = (
        ctx.salloc("g_gsl", [16, TS], F32),
        ctx.salloc("g_gtok", [128, 16], F32),
        ctx.salloc("g_mx", [128, 1], F32),
        ctx.salloc("g_nmx", [128, 1], F32),
        ctx.salloc("g_oh", [128, 16], F32),
        ctx.salloc("g_cand", [128, 16], F32),
        ctx.salloc("g_eidx", [128, 1], F32),
        ctx.salloc("g_ohb", [128, 16], BF16),
        ctx.salloc("g_eg", [128, 16], F32),
        ctx.salloc("g_seg", [128, 1], F32),
        ctx.salloc("g_prob", [128, 1], F32),
        ctx.salloc("g_rnk", [128, 16], F32),
        ctx.salloc("g_rsel", [128, 1], F32),
        ctx.salloc("g_offs", [1, 16], F32),
        ctx.salloc("g_probsb", [128, 32], F32),
        ctx.salloc("g_eidxsb", [128, 32], F32),
        ctx.salloc("g_lranksb", [128, 32], F32),
        ctx.salloc("g_routet", [128, 4], F32),
    )
    gate_cb = emit_gate_router(ctx, gwT, negs, gate_bufs, key)
    emit_ln(ctx, x1_in, xnfm, key, gate_cb=gate_cb)
    offs = gate_bufs[13]
    probsb, eidxsb, lranksb, routet = gate_bufs[14], gate_bufs[15], gate_bufs[16], gate_bufs[17]
    p.op("sync", lambda e: e.dma_start(out=counts_out, in_=offs[:, :]),
         reads=[("goffs", key)], writes=[("countsout",)], dma=True)
    fence2 = p.op("gpsimd", lambda e: e.nop(),
                  reads=[("xnfm", key, c, q) for c in range(8) for q in range(8)],
                  writes=[("F2",)])
    for c in range(8):
        for q in range(8):
            p.lastw[("xnfm-src", key, c, q)] = fence2
            p.lastw[("xnfm-src2", key, c, q)] = fence2

    ctx.push_scope()
    s1b = len(p.ops)
    for tt in range(32):
        p.op("vector", lambda e, tt=tt: e.memset(routet[:, :], 0.0),
             writes=[("groutet",)])
        p.op("vector", lambda e, tt=tt: e.tensor_copy(routet[:, 0:1], eidxsb[:, tt:tt + 1]),
             reads=[("geidxsb", key, tt)], writes=[("groutet",)])
        p.op("vector", lambda e, tt=tt: e.tensor_copy(routet[:, 1:2], lranksb[:, tt:tt + 1]),
             reads=[("glrank", key, tt)], writes=[("groutet",)])
        p.op("vector", lambda e, tt=tt: e.tensor_copy(routet[:, 2:3], probsb[:, tt:tt + 1]),
             reads=[("gprob", key, tt)], writes=[("groutet",)])
        p.op("sync", lambda e, tt=tt: e.dma_start(
            out=route_out[tt * 128:(tt + 1) * 128, :], in_=routet[:, :]),
            reads=[("groutet",)], writes=[("routeout", tt)], dma=True)
    xtok = ctx.salloc("xtok", [128, 1024], BF16, is_transpose_dest=True)
    xaug = ctx.salloc("xaug", [128, 1032], BF16)
    for tt in range(32):
        p.op("sync", lambda e, tt=tt: e.dma_start_transpose(
            xtok[:, :], xnfm[0:D, tt * 128:(tt + 1) * 128]),
            reads=[("xnfm-src", key, 0, 0)], writes=[("xtok",)], dma=True)
        p.op("vector", lambda e: e.tensor_copy(xaug[:, 0:1024], xtok[:, :]),
             reads=[("xtok",)], writes=[("xaug",)])
        p.op("vector", lambda e: e.memset(xaug[:, 1024:1032], 0.0),
             writes=[("xaug",)])
        p.op("vector", lambda e, tt=tt: e.tensor_copy(xaug[:, 1024:1025],
                                                      probsb[:, tt:tt + 1]),
             reads=[("gprob", key, tt)], writes=[("xaug",)])
        p.op("sync", lambda e, tt=tt: e.dma_start(
            out=xn1aug_out[tt * 128:(tt + 1) * 128, :], in_=xaug[:, :]),
            reads=[("xaug",)], writes=[("xn1augout", tt)], dma=True)
    s2b = len(p.ops)

    def mk_l1src(wb, castnm):
        def f(c, tile, wkey):
            for h in range(H):
                p.op("sync", lambda e, c=c, h=h, tile=tile: e.dma_start(
                    out=tile[:, h * 64:(h + 1) * 64],
                    in_=wb[h * D + c * 128: h * D + (c + 1) * 128, :]),
                    reads=[("cast", castnm)], writes=[wkey], dma=True)
        return f

    ctx.push_scope()
    vptiles = [ctx.salloc(f"vp1_{tt}", [128, H * 65], BF16) for tt in range(T // 128)]
    emit_qkv(ctx, xnfm, qfm, kfm, mk_l1src(wqb, "wq1"), mk_l1src(wkb, "wk1"),
             mk_l1src(wvb, "wv1"), vptiles, key)
    p.interleave(s1b, s2b)
    fq = p.op("gpsimd", lambda e: e.nop(),
              reads=[("qfm", key, hp, ts2) for hp in range(8) for ts2 in range(8)],
              writes=[("F3",)])
    p.lastw[("qfm-all", key)] = fq
    fk = p.op("gpsimd", lambda e: e.nop(),
              reads=[("kfm", key, hp, ts2) for hp in range(8) for ts2 in range(8)],
              writes=[("F4",)])
    p.lastw[("kfm-all", key)] = fk
    emit_flash(ctx, qfm, kfm, vptiles, avfm, causal=False, key=key)
    ctx.pop_scope()
    ctx.pop_scope()

    emit_epilogue_l1(ctx, x1_in, avfm, wob, out1p, key)

    p.op("sync", lambda e: e.nop(),
         reads=[("out1p", key, ds, ts2) for ds in range(8) for ts2 in range(8)] +
               [("countsout",)] + [("routeout", tt) for tt in range(32)] +
               [("xn1augout", tt) for tt in range(32)],
         writes=[("END",)])


def build_launch_AB():
    nc = bass.Bass("TRN2", target_bir_lowering=False)
    ctx = Ctx(nc)
    ctx.c_dram = make_consts(nc)
    x_in = nc.dram_tensor("x", [T, D], F32, kind="ExternalInput").ap()
    wq_in = nc.dram_tensor("wq", [D, D], F32, kind="ExternalInput").ap()
    wk_in = nc.dram_tensor("wk", [D, D], F32, kind="ExternalInput").ap()
    wv_in = nc.dram_tensor("wv", [D, D], F32, kind="ExternalInput").ap()
    wo_in = nc.dram_tensor("wo", [H * D, DH], F32, kind="ExternalInput").ap()
    w1_in = nc.dram_tensor("w1", [FF, D], F32, kind="ExternalInput").ap()
    b1_in = nc.dram_tensor("b1", [32, 128], F32, kind="ExternalInput").ap()
    w2_in = nc.dram_tensor("w2", [D, FF], F32, kind="ExternalInput").ap()
    b2_in = nc.dram_tensor("b2", [1, D], F32, kind="ExternalInput").ap()
    wq1_in = nc.dram_tensor("wq1", [H * D, DH], F32, kind="ExternalInput").ap()
    wk1_in = nc.dram_tensor("wk1", [H * D, DH], F32, kind="ExternalInput").ap()
    wv1_in = nc.dram_tensor("wv1", [H * D, DH], F32, kind="ExternalInput").ap()
    wo1_in = nc.dram_tensor("wo1", [D, D], F32, kind="ExternalInput").ap()
    gw_in = nc.dram_tensor("gw", [NE, D], F32, kind="ExternalInput").ap()
    out1p = nc.dram_tensor("out1pfm", [D, T], F32, kind="ExternalOutput").ap()
    xn1aug_out = nc.dram_tensor("xn1aug", [T, 1032], BF16, kind="ExternalOutput").ap()
    route_out = nc.dram_tensor("route", [T, 4], F32, kind="ExternalOutput").ap()
    counts_out = nc.dram_tensor("counts", [1, 16], F32, kind="ExternalOutput").ap()
    x1fm = nc.dram_tensor("x1fm", [D, T], F32).ap()
    ctx.PS = nc.alloc_psum_tensor("PS", [128, 4096], F32)
    p = ctx.p
    load_consts(ctx, ["ident_bf", "ident_f32", "tri01_0", "tri01_1", "tri01_2",
                      "tri01_3", "ones_col_f32", "ones_1x128_f32",
                      "iota16", "triu_incl", "ones_col_bf"])
    emit_layer0(ctx, x_in, wq_in, wk_in, wv_in, wo_in, w1_in, b1_in, w2_in,
                b2_in, x1fm)
    fx1 = p.op("gpsimd", lambda e: e.nop(),
               reads=[("x1fm", ("l0",), ds, ts2) for ds in range(8) for ts2 in range(8)],
               writes=[("FX1",)])
    for c in range(8):
        for q in range(8):
            p.lastw[("xfm-src", ("l1",), c, q)] = fx1
            p.lastw[("x1-src", ("l1",), c, q)] = fx1
    emit_layer1(ctx, x1fm, wq1_in, wk1_in, wv1_in, wo1_in, gw_in, out1p,
                xn1aug_out, route_out, counts_out)
    p.finalize()
    return nc


def build_launch_C():
    nc = bass.Bass("TRN2", target_bir_lowering=False)
    ctx = Ctx(nc)
    ctx.c_dram = make_consts(nc)
    NTOK = NCORES * T
    xn_all = nc.dram_tensor("xn_all", [NTOK, 1032], BF16, kind="ExternalInput").ap()
    route_all = nc.dram_tensor("route_all", [NTOK, 4], F32, kind="ExternalInput").ap()
    counts_all = nc.dram_tensor("counts_all", [8, 16], F32, kind="ExternalInput").ap()
    w1e_in = nc.dram_tensor("w1e", [EPP * D, FF], F32, kind="ExternalInput").ap()
    w2e_in = nc.dram_tensor("w2e", [EPP * FF, D], F32, kind="ExternalInput").ap()
    mybase_in = nc.dram_tensor("mybase", [1, 1], F32, kind="ExternalInput").ap()
    eout = nc.dram_tensor("expert_my", [EPP * CAP, D], BF16, kind="ExternalOutput").ap()
    disp = nc.dram_tensor("disp", [EPP * CAP, 1032], BF16).ap()
    ctx.PS = nc.alloc_psum_tensor("PS", [128, 4096], F32)
    p = ctx.p
    load_consts(ctx, ["ident_f32", "ones_1x128_f32", "iota16", "striu8"])
    key = ("mc",)
    PS = ctx.PS

    ctx.push_scope()
    cts = ctx.salloc("cts", [8, 16], F32)
    offs8 = ctx.salloc("offs8", [8, 16], F32)
    p.op("sync", lambda e: e.dma_start(out=cts[:, :], in_=counts_all),
         writes=[("mcts",)], dma=True)
    st8 = ctx.salloc("st8", [8, 8], F32)
    p.op("sync", lambda e: e.dma_start(out=st8[:, :], in_=ctx.c_dram["striu8"].ap()),
         writes=[("mst8",)], dma=True)
    p.op("tensor", lambda e: e.matmul(PS[0:8, 0:16], st8[:, :], cts[:, :],
                                      start=True, stop=True),
         reads=[("mst8",), ("mcts",)], writes=[("ps", 0)])
    p.op("vector", lambda e: e.tensor_copy(offs8[:, :], PS[0:8, 0:16]),
         reads=[("ps", 0)], writes=[("moffs8",)])
    mybase = ctx.salloc("mybase_s", [1, 1], F32)
    p.op("sync", lambda e: e.dma_start(out=mybase[:, :], in_=mybase_in),
         writes=[("mybase",)], dma=True)
    mybB = ctx.salloc("mybB", [128, 1], F32)
    p.op("tensor", lambda e: e.matmul(PS[0:128, 512:513], ctx.c["ones_1x128_f32"][:, :],
                                      mybase[:, :], start=True, stop=True),
         reads=[("mybase",), ("c", "ones_1x128_f32")], writes=[("ps", 1)])
    p.op("vector", lambda e: e.tensor_copy(mybB[:, :], PS[0:128, 512:513]),
         reads=[("ps", 1)], writes=[("mybB",)])
    ident8 = ctx.salloc("ident8", [8, 8], F32)
    p.op("sync", lambda e: e.dma_start(out=ident8[:, :],
                                       in_=ctx.c_dram["ident_f32"].ap()[0:8, 0:8]),
         writes=[("mid8",)], dma=True)
    zt = ctx.salloc("zt", [128, 1032], BF16)
    p.op("vector", lambda e: e.memset(zt[:, :], 0.0), writes=[("mzt",)])
    for rt in range(EPP * CAP // 128):
        p.op("sync", lambda e, rt=rt: e.dma_start(
            out=disp[rt * 128:(rt + 1) * 128, :], in_=zt[:, :]),
            reads=[("mzt",)], writes=[("mdispz", rt)], dma=True)
    obc = ctx.salloc("obc", [128, 16], F32)
    NPAR = 4
    PENG = ["vector", "vector", "vector", "vector"]
    rt_t = [ctx.salloc(f"rt_t{i}", [128, 4], F32) for i in range(NPAR)]
    ohx = [ctx.salloc(f"ohx{i}", [128, 16], F32) for i in range(NPAR)]
    offtok = [ctx.salloc(f"offtok{i}", [128, 1], F32) for i in range(NPAR)]
    grank = [ctx.salloc(f"grank{i}", [128, 1], F32) for i in range(NPAR)]
    valid = [ctx.salloc(f"valid{i}", [128, 1], F32) for i in range(NPAR)]
    badv = [ctx.salloc(f"badv{i}", [128, 1], F32) for i in range(NPAR)]
    destf = [ctx.salloc(f"destf{i}", [128, 1], F32) for i in range(NPAR)]
    desti = [ctx.salloc(f"desti{i}", [128, 1], I32) for i in range(NPAR)]
    xrow = [ctx.salloc(f"xrow{i}", [128, 1032], BF16) for i in range(NPAR)]
    bcreg = nc.alloc_register(mybir.EngineType.Pool, "bcreg")
    p.op("gpsimd", lambda e: e.reg_mov(bcreg, EPP * CAP), writes=[("bcreg",)])
    for cb in range(8):
        p.op("tensor", lambda e, cb=cb: e.matmul(
            PS[0:1, 0:16], ident8[:, cb:cb + 1], offs8[:, :], start=True, stop=True),
            reads=[("mid8",), ("moffs8",)], writes=[("ps", 0)])
        p.op("vector", lambda e: e.tensor_copy(obc[0:1, :], PS[0:1, 0:16]),
             reads=[("ps", 0)], writes=[("mobcrow",)])
        p.op("tensor", lambda e: e.matmul(PS[0:128, 0:16], ctx.c["ones_1x128_f32"][:, :],
                                          obc[0:1, :], start=True, stop=True),
             reads=[("mobcrow",), ("c", "ones_1x128_f32")], writes=[("ps", 0)])
        p.op("vector", lambda e: e.tensor_copy(obc[:, :], PS[0:128, 0:16]),
             reads=[("ps", 0)], writes=[("mobc",)])
        for tt in range(32):
            i = tt % NPAR
            E = PENG[i]
            base = cb * T + tt * 128
            p.op("sync", lambda e, base=base, i=i: e.dma_start(
                out=rt_t[i][:, :], in_=route_all[base:base + 128, :]),
                writes=[("mrt", i)], dma=True)
            p.op(E, lambda e, i=i: e.tensor_scalar(ohx[i][:, :], ctx.c["iota16"][:, :],
                                                   rt_t[i][:, 0:1], None, ALU.is_equal),
                reads=[("mrt", i), ("c", "iota16")], writes=[("mohx", i)])
            p.op(E, lambda e, i=i: e.tensor_mul(ohx[i][:, :], ohx[i][:, :], obc[:, :]),
                reads=[("mohx", i), ("mobc",)], writes=[("mohx", i)])
            p.op("vector", lambda e, i=i: e.reduce_sum(offtok[i][:, :], ohx[i][:, :], axis=AX.X),
                 reads=[("mohx", i)], writes=[("mofftok", i)])
            p.op(E, lambda e, i=i: e.tensor_add(grank[i][:, :], rt_t[i][:, 1:2], offtok[i][:, :]),
                 reads=[("mrt", i), ("mofftok", i)], writes=[("mgrank", i)])
            p.op(E, lambda e, i=i: e.tensor_scalar(valid[i][:, :], grank[i][:, :], float(CAP),
                                                   None, ALU.is_lt),
                 reads=[("mgrank", i)], writes=[("mvalid", i)])
            p.op(E, lambda e, i=i: e.tensor_scalar(destf[i][:, :], rt_t[i][:, 0:1], float(CAP),
                                                   None, ALU.mult),
                 reads=[("mrt", i)], writes=[("mdestf", i)])
            p.op(E, lambda e, i=i: e.tensor_add(destf[i][:, :], destf[i][:, :], grank[i][:, :]),
                 reads=[("mdestf", i), ("mgrank", i)], writes=[("mdestf", i)])
            p.op(E, lambda e, i=i: e.tensor_sub(destf[i][:, :], destf[i][:, :], mybB[:, :]),
                 reads=[("mdestf", i), ("mybB",)], writes=[("mdestf", i)])
            p.op(E, lambda e, i=i: e.tensor_scalar(badv[i][:, :], destf[i][:, :], 0.0,
                                                   None, ALU.is_ge),
                 reads=[("mdestf", i)], writes=[("mbadv", i)])
            p.op(E, lambda e, i=i: e.tensor_mul(valid[i][:, :], valid[i][:, :], badv[i][:, :]),
                 reads=[("mvalid", i), ("mbadv", i)], writes=[("mvalid", i)])
            p.op(E, lambda e, i=i: e.tensor_mul(destf[i][:, :], destf[i][:, :], valid[i][:, :]),
                 reads=[("mdestf", i), ("mvalid", i)], writes=[("mdestf", i)])
            p.op(E, lambda e, i=i: e.tensor_scalar(badv[i][:, :], valid[i][:, :], -1e9, 1e9,
                                                   ALU.mult, ALU.add),
                 reads=[("mvalid", i)], writes=[("mbadv", i)])
            p.op(E, lambda e, i=i: e.tensor_add(destf[i][:, :], destf[i][:, :], badv[i][:, :]),
                 reads=[("mdestf", i), ("mbadv", i)], writes=[("mdestf", i)])
            p.op(E, lambda e, i=i: e.tensor_copy(desti[i][:, :], destf[i][:, :]),
                 reads=[("mdestf", i)], writes=[("mdesti", i)])
            p.op("sync", lambda e, base=base, i=i: e.dma_start(
                out=xrow[i][:, :], in_=xn_all[base:base + 128, :]),
                writes=[("mxrow", i)], dma=True)
            p.op("gpsimd", lambda e, i=i: e.indirect_dma_start(
                disp, IndirectOffsetOnAxis(desti[i][:, 0:1], 0), xrow[i][:, :], None,
                bounds_check=bcreg, oob_is_err=False),
                reads=[("mdesti", i), ("mxrow", i), ("bcreg",)] +
                      ([("mdispz", rt) for rt in range(EPP * CAP // 128)] if cb == 0 and tt == 0 else []),
                writes=[("mdisp",)], dma=True)
    fence = p.op("gpsimd", lambda e: e.nop(), reads=[("mdisp",)], writes=[("mdispF",)])
    ctx.pop_scope()

    ctx.push_scope()
    dispT = [ctx.salloc(f"dispT_{c}", [128, CAP], BF16, is_transpose_dest=True)
             for c in range(8)]
    w1r = [ctx.salloc(f"w1r_{c}", [128, FF], BF16) for c in range(8)]
    probc = ctx.salloc("probc", [128, 8], BF16)
    w2b = [ctx.salloc(f"w2b_{i}", [128, 1024], BF16) for i in range(4)]
    hsl = [ctx.salloc(f"mh_{f}", [128, TS], BF16) for f in range(32)]
    gs = ctx.salloc("gs", [128, TS], F32)
    gu = ctx.salloc("gu", [128, TS], F32)
    gt = ctx.salloc("gt", [128, TS], F32)
    gw_ = ctx.salloc("gw_", [128, TS], F32)
    ga = ctx.salloc("ga", [128, TS], F32)
    outb = [ctx.salloc(f"outb_{i}", [128, TS], BF16) for i in range(8)]
    probv = [ctx.salloc(f"probv_{i}", [128, 1], F32) for i in range(4)]
    C0 = 0.044715
    C1 = math.sqrt(2.0 / math.pi)
    for ex in range(EPP):
        for c in range(8):
            p.op("sync", lambda e, ex=ex, c=c: e.dma_start_transpose(
                dispT[c][:, :], disp[ex * CAP:(ex + 1) * CAP, c * 128:(c + 1) * 128]),
                extra_deps=[fence], writes=[("mdispT", c)], dma=True)
            p.op("gpsimd", lambda e, ex=ex, c=c: e.dma_start(
                out=w1r[c][:, :], in_=w1e_in[ex * D + c * 128:ex * D + (c + 1) * 128, :]),
                writes=[("mw1r", c)], dma=True)
        for capb in range(CAP // TS):
            for fc in range(32):
                bank = 4 + fc % 2
                for c in range(8):
                    p.op("tensor", lambda e, c=c, fc=fc, capb=capb, bank=bank: e.matmul(
                        PS[0:128, bank * 512:bank * 512 + TS],
                        w1r[c][:, fc * 128:(fc + 1) * 128],
                        dispT[c][:, capb * TS:(capb + 1) * TS],
                        start=(c == 0), stop=(c == 7)),
                        reads=[("mw1r", c), ("mdispT", c)], writes=[("ps", bank)])
                p.op("scalar", lambda e, bank=bank: e.activation(
                    gs[:, :], PS[0:128, bank * 512:bank * 512 + TS], AF.Square),
                    reads=[("ps", bank)], writes=[("mgs",)])
                p.op("vector", lambda e: e.tensor_scalar(gu[:, :], gs[:, :], C0, 1.0,
                                                         ALU.mult, ALU.add),
                     reads=[("mgs",)], writes=[("mgu",)])
                p.op("vector", lambda e, bank=bank: e.tensor_mul(
                    gt[:, :], gu[:, :], PS[0:128, bank * 512:bank * 512 + TS]),
                    reads=[("mgu",), ("ps", bank)], writes=[("mgt",)])
                p.op("scalar", lambda e: e.activation(gw_[:, :], gt[:, :], AF.Tanh,
                                                      scale=C1),
                     reads=[("mgt",)], writes=[("mgw",)])
                p.op("vector", lambda e: e.tensor_scalar(ga[:, :], gw_[:, :], 0.5, 0.5,
                                                         ALU.mult, ALU.add),
                     reads=[("mgw",)], writes=[("mga",)])
                p.op("vector", lambda e, fc=fc, bank=bank: e.tensor_mul(
                    hsl[fc][:, :], ga[:, :], PS[0:128, bank * 512:bank * 512 + TS]),
                    reads=[("mga",), ("ps", bank)], writes=[("mh", fc)])
            for sub in range(4):
                ct = capb * 4 + sub
                p.op("sync", lambda e, ex=ex, ct=ct, sub=sub: e.dma_start(
                    out=probc[:, :], in_=disp[ex * CAP + ct * 128:ex * CAP + (ct + 1) * 128,
                                              1024:1032]),
                    extra_deps=[fence], writes=[("mprobc",)], dma=True)
                p.op("vector", lambda e, sub=sub: e.tensor_copy(probv[sub][:, :], probc[:, 0:1]),
                     reads=[("mprobc",)], writes=[("mprobv", sub)])
            for fc in range(32):
                wi = fc % 4
                p.op("gpsimd", lambda e, ex=ex, fc=fc, wi=wi: e.dma_start(
                    out=w2b[wi][:, :],
                    in_=w2e_in[ex * FF + fc * 128:ex * FF + (fc + 1) * 128, :]),
                    writes=[("mw2b", wi)], dma=True)
                for sub in range(4):
                    for half in range(2):
                        bank = sub * 2 + half
                        p.op("tensor", lambda e, fc=fc, sub=sub, half=half, bank=bank, wi=wi: e.matmul(
                            PS[0:128, bank * 512:bank * 512 + TS],
                            hsl[fc][:, sub * 128:(sub + 1) * 128],
                            w2b[wi][:, half * 512:(half + 1) * 512],
                            start=(fc == 0), stop=(fc == 31)),
                            reads=[("mh", fc), ("mw2b", wi)], writes=[("ps", bank)])
            for sub in range(4):
                ct = capb * 4 + sub
                for half in range(2):
                    bank = sub * 2 + half
                    p.op("vector", lambda e, sub=sub, half=half, bank=bank: e.tensor_scalar(
                        outb[sub * 2 + half][:, :],
                        PS[0:128, bank * 512:bank * 512 + TS],
                        probv[sub][:, 0:1], None, ALU.mult),
                        reads=[("ps", bank), ("mprobv", sub)], writes=[("moutb", sub, half)])
                p.op("sync", lambda e, ex=ex, ct=ct, sub=sub: e.dma_start(
                    out=eout[ex * CAP + ct * 128:ex * CAP + (ct + 1) * 128, 0:512],
                    in_=outb[sub * 2][:, :]),
                    reads=[("moutb", sub, 0)], writes=[("meout", ex, ct)], dma=True)
                p.op("sync", lambda e, ex=ex, ct=ct, sub=sub: e.dma_start(
                    out=eout[ex * CAP + ct * 128:ex * CAP + (ct + 1) * 128, 512:1024],
                    in_=outb[sub * 2 + 1][:, :]),
                    reads=[("moutb", sub, 1)], writes=[("meout", ex, ct)], dma=True)
    ctx.pop_scope()
    p.op("sync", lambda e: e.nop(),
         reads=[("meout", ex, ct) for ex in range(EPP) for ct in range(CAP // 128)],
         writes=[("END",)])
    p.finalize()
    return nc


def build_launch_D():
    nc = bass.Bass("TRN2", target_bir_lowering=False)
    ctx = Ctx(nc)
    ctx.c_dram = make_consts(nc)
    NSLOT = NE * CAP
    expert_all = nc.dram_tensor("expert_all", [NSLOT, D], BF16, kind="ExternalInput").ap()
    route_in = nc.dram_tensor("route", [T, 4], F32, kind="ExternalInput").ap()
    counts_all = nc.dram_tensor("counts_all", [8, 16], F32, kind="ExternalInput").ap()
    out1p_in = nc.dram_tensor("out1pfm", [D, T], F32, kind="ExternalInput").ap()
    ecore_in = nc.dram_tensor("ecore", [8, 1], F32, kind="ExternalInput").ap()
    out_tok = nc.dram_tensor("out", [T, D], F32, kind="ExternalOutput").ap()
    moetok = nc.dram_tensor("moetok", [T, D], BF16).ap()
    out2fm = nc.dram_tensor("out2fm", [D, T], F32).ap()
    ctx.PS = nc.alloc_psum_tensor("PS", [128, 3584], F32)
    p = ctx.p
    load_consts(ctx, ["ident_f32", "ones_1x128_f32", "iota16", "striu8"])
    PS = ctx.PS
    key = ("md",)

    ctx.push_scope()
    cts = ctx.salloc("cts", [8, 16], F32)
    st8 = ctx.salloc("st8", [8, 8], F32)
    ecore = ctx.salloc("ecore_s", [8, 1], F32)
    obc = ctx.salloc("obc", [128, 16], F32)
    p.op("sync", lambda e: e.dma_start(out=cts[:, :], in_=counts_all),
         writes=[("dcts",)], dma=True)
    p.op("sync", lambda e: e.dma_start(out=st8[:, :], in_=ctx.c_dram["striu8"].ap()),
         writes=[("dst8",)], dma=True)
    p.op("sync", lambda e: e.dma_start(out=ecore[:, :], in_=ecore_in),
         writes=[("dec",)], dma=True)
    p.op("tensor", lambda e: e.matmul(PS[0:8, 0:16], st8[:, :], cts[:, :],
                                      start=True, stop=True),
         reads=[("dst8",), ("dcts",)], writes=[("ps", 0)])
    offs8 = ctx.salloc("offs8", [8, 16], F32)
    p.op("vector", lambda e: e.tensor_copy(offs8[:, :], PS[0:8, 0:16]),
         reads=[("ps", 0)], writes=[("doffs8",)])
    p.op("tensor", lambda e: e.matmul(PS[0:1, 0:16], ecore[:, :], offs8[:, :],
                                      start=True, stop=True),
         reads=[("dec",), ("doffs8",)], writes=[("ps", 0)])
    myoff = ctx.salloc("myoff", [1, 16], F32)
    p.op("vector", lambda e: e.tensor_copy(myoff[:, :], PS[0:1, 0:16]),
         reads=[("ps", 0)], writes=[("dmyoff",)])
    p.op("tensor", lambda e: e.matmul(PS[0:128, 0:16], ctx.c["ones_1x128_f32"][:, :],
                                      myoff[:, :], start=True, stop=True),
         reads=[("dmyoff",), ("c", "ones_1x128_f32")], writes=[("ps", 0)])
    p.op("vector", lambda e: e.tensor_copy(obc[:, :], PS[0:128, 0:16]),
         reads=[("ps", 0)], writes=[("dobc",)])
    bcreg = nc.alloc_register(mybir.EngineType.Pool, "bcreg")
    p.op("gpsimd", lambda e: e.reg_mov(bcreg, NSLOT), writes=[("bcreg",)])
    NPAR = 4
    PENG = ["vector", "vector", "vector", "vector"]
    rt_t = [ctx.salloc(f"rt_t{i}", [128, 4], F32) for i in range(NPAR)]
    ohx = [ctx.salloc(f"ohx{i}", [128, 16], F32) for i in range(NPAR)]
    offtok = [ctx.salloc(f"offtok{i}", [128, 1], F32) for i in range(NPAR)]
    grank = [ctx.salloc(f"grank{i}", [128, 1], F32) for i in range(NPAR)]
    valid = [ctx.salloc(f"valid{i}", [128, 1], F32) for i in range(NPAR)]
    srcf = [ctx.salloc(f"srcf{i}", [128, 1], F32) for i in range(NPAR)]
    srci = [ctx.salloc(f"srci{i}", [128, 1], I32) for i in range(NPAR)]
    mrow = [ctx.salloc(f"mrow{i}", [128, D], BF16) for i in range(NPAR)]
    for tt in range(32):
        i = tt % NPAR
        E = PENG[i]
        p.op("sync", lambda e, tt=tt, i=i: e.dma_start(
            out=rt_t[i][:, :], in_=route_in[tt * 128:(tt + 1) * 128, :]),
            writes=[("drt", i)], dma=True)
        p.op(E, lambda e, i=i: e.tensor_scalar(ohx[i][:, :], ctx.c["iota16"][:, :],
                                               rt_t[i][:, 0:1], None, ALU.is_equal),
            reads=[("drt", i), ("c", "iota16")], writes=[("dohx", i)])
        p.op(E, lambda e, i=i: e.tensor_mul(ohx[i][:, :], ohx[i][:, :], obc[:, :]),
            reads=[("dohx", i), ("dobc",)], writes=[("dohx", i)])
        p.op("vector", lambda e, i=i: e.reduce_sum(offtok[i][:, :], ohx[i][:, :], axis=AX.X),
             reads=[("dohx", i)], writes=[("dofftok", i)])
        p.op(E, lambda e, i=i: e.tensor_add(grank[i][:, :], rt_t[i][:, 1:2], offtok[i][:, :]),
             reads=[("drt", i), ("dofftok", i)], writes=[("dgrank", i)])
        p.op(E, lambda e, i=i: e.tensor_scalar(valid[i][:, :], grank[i][:, :], float(CAP),
                                               None, ALU.is_lt),
             reads=[("dgrank", i)], writes=[("dvalid", i)])
        p.op(E, lambda e, i=i: e.tensor_scalar(srcf[i][:, :], rt_t[i][:, 0:1], float(CAP),
                                               None, ALU.mult),
             reads=[("drt", i)], writes=[("dsrcf", i)])
        p.op(E, lambda e, i=i: e.tensor_add(srcf[i][:, :], srcf[i][:, :], grank[i][:, :]),
             reads=[("dsrcf", i), ("dgrank", i)], writes=[("dsrcf", i)])
        p.op(E, lambda e, i=i: e.tensor_mul(srcf[i][:, :], srcf[i][:, :], valid[i][:, :]),
             reads=[("dsrcf", i), ("dvalid", i)], writes=[("dsrcf", i)])
        p.op(E, lambda e, i=i: e.tensor_scalar(ohx[i][:, 0:1], valid[i][:, :], -1e9, 1e9,
                                               ALU.mult, ALU.add),
             reads=[("dvalid", i)], writes=[("dbadv", i)])
        p.op(E, lambda e, i=i: e.tensor_add(srcf[i][:, :], srcf[i][:, :], ohx[i][:, 0:1]),
             reads=[("dsrcf", i), ("dbadv", i)], writes=[("dsrcf", i)])
        p.op(E, lambda e, i=i: e.tensor_copy(srci[i][:, :], srcf[i][:, :]),
             reads=[("dsrcf", i)], writes=[("dsrci", i)])
        p.op(E, lambda e, i=i: e.memset(mrow[i][:, :], 0.0), writes=[("dmrow", i)])
        p.op("gpsimd", lambda e, i=i: e.indirect_dma_start(
            mrow[i][:, :], None, expert_all, IndirectOffsetOnAxis(srci[i][:, 0:1], 0),
            bounds_check=bcreg, oob_is_err=False),
            reads=[("dsrci", i), ("dmrow", i), ("bcreg",)], writes=[("dmrow", i)], dma=True)
        p.op("sync", lambda e, tt=tt, i=i: e.dma_start(
            out=moetok[tt * 128:(tt + 1) * 128, :], in_=mrow[i][:, :]),
            reads=[("dmrow", i)], writes=[("dmoetok", tt)], dma=True)
    fence = p.op("gpsimd", lambda e: e.nop(),
                 reads=[("dmoetok", tt) for tt in range(32)], writes=[("dmoeF",)])
    ctx.pop_scope()

    ctx.push_scope()
    moefm = ctx.salloc("moefm", [128, T], BF16, is_transpose_dest=True)
    x1t = ctx.salloc("x1t", [128, TS], F32)
    osum = ctx.salloc("osum", [128, TS], F32)
    for c in range(8):
        p.op("sync", lambda e, c=c: e.dma_start_transpose(
            moefm[:, :], moetok[0:T, c * 128:(c + 1) * 128]),
            extra_deps=[fence], writes=[("dmoefm",)], dma=True)
        for ts in range(NTS):
            p.op("sync", lambda e, c=c, ts=ts: e.dma_start(
                out=x1t[:, :], in_=out1p_in[c * 128:(c + 1) * 128, ts * TS:(ts + 1) * TS]),
                writes=[("dx1t",)], dma=True)
            p.op("vector", lambda e, ts=ts: e.tensor_add(
                osum[:, :], x1t[:, :], moefm[:, ts * TS:(ts + 1) * TS]),
                reads=[("dx1t",), ("dmoefm",)], writes=[("dosum",)])
            p.op("sync", lambda e, c=c, ts=ts: e.dma_start(
                out=out2fm[c * 128:(c + 1) * 128, ts * TS:(ts + 1) * TS], in_=osum[:, :]),
                reads=[("dosum",)], writes=[("dout2", c, ts)], dma=True)
    ctx.pop_scope()
    f2 = p.op("gpsimd", lambda e: e.nop(),
              reads=[("dout2", c, ts) for c in range(8) for ts in range(8)],
              writes=[("dout2F",)])
    for c in range(8):
        p.lastw[("f2t-src", ("o",), c)] = f2
    emit_fm_to_tok_f32(ctx, out2fm, out_tok, ("o",))
    p.op("sync", lambda e: e.nop(),
         reads=[("f2t-out", ("o",), g, c) for g in range(8) for c in range(8)],
         writes=[("END",)])
    p.finalize()
    return nc


def emit_layer0(ctx, x_in, wq_in, wk_in, wv_in, wo_in, w1_in, b1_in, w2_in,
                b2_in, x1fm_out):
    nc, p = ctx.nc, ctx.p
    xfm = nc.dram_tensor("xfm", [D, T], F32).ap()
    xnfm = nc.dram_tensor("xnfm", [D, T], BF16).ap()
    qfm = nc.dram_tensor("qfm", [D, T], BF16).ap()
    kfm = nc.dram_tensor("kfm", [D, T], BF16).ap()
    avfm = nc.dram_tensor("avfm", [D, T], BF16).ap()
    wqb = nc.dram_tensor("wqb", [D, D], BF16).ap()
    wkb = nc.dram_tensor("wkb", [D, D], BF16).ap()
    wvb = nc.dram_tensor("wvb", [D, D], BF16).ap()
    w1qb = nc.dram_tensor("w1qb", [FF, D], BF16).ap()
    w2qb = nc.dram_tensor("w2qb", [D, FF], BF16).ap()
    w2qtb = nc.dram_tensor("w2qtb", [FF, D], BF16).ap()

    for wsrc, dst, nm in [(wq_in, wqb, "wq"), (wk_in, wkb, "wk"), (wv_in, wvb, "wv")]:
        p.op("gpsimd", lambda e, wsrc=wsrc, dst=dst: e.dma_start(out=dst, in_=wsrc),
             writes=[("cast", nm)], dma=True)

    ctx.push_scope()
    s1 = len(p.ops)
    emit_quantize(ctx, w1_in, FF, D, w1qb, ("l0w1",), scoped=False)
    emit_quantize(ctx, w2_in, D, FF, w2qb, ("l0w2",), wqT_dram=w2qtb, scoped=False)
    s2 = len(p.ops)
    emit_x_to_fm(ctx, x_in, xfm, ("x2f",))
    fence = p.op("gpsimd", lambda e: e.nop(),
                 reads=[("xfm", ("x2f",), c, g) for c in range(8) for g in range(8)],
                 writes=[("F1",)])
    for c in range(8):
        for q in range(8):
            p.lastw[("xfm-src", ("l0",), c, q)] = fence
            p.lastw[("xfm-src3", ("l0",), c, q)] = fence

    emit_ln(ctx, xfm, xnfm, ("l0",))
    p.interleave(s1, s2)
    ctx.pop_scope()
    fence2 = p.op("gpsimd", lambda e: e.nop(),
                  reads=[("xnfm", ("l0",), c, q) for c in range(8) for q in range(8)],
                  writes=[("F2",)])
    for c in range(8):
        for q in range(8):
            p.lastw[("xnfm-src", ("l0",), c, q)] = fence2
            p.lastw[("xnfm-src2", ("l0",), c, q)] = fence2

    def mk_tsrc(wb, castnm):
        def f(c, tile, wkey):
            p.op("sync", lambda e, c=c, tile=tile: e.dma_start_transpose(
                tile[:, :], wb[0:D, c * 128:(c + 1) * 128]),
                reads=[("cast", castnm)], writes=[wkey], dma=True)
        return f

    ctx.push_scope()
    vptiles = [ctx.salloc(f"vp_{tt}", [128, H * 65], BF16) for tt in range(T // 128)]
    emit_qkv(ctx, xnfm, qfm, kfm, mk_tsrc(wqb, "wq"), mk_tsrc(wkb, "wk"),
             mk_tsrc(wvb, "wv"), vptiles, ("l0",))
    fq = p.op("gpsimd", lambda e: e.nop(),
              reads=[("qfm", ("l0",), hp, ts2) for hp in range(8) for ts2 in range(8)],
              writes=[("F3",)])
    p.lastw[("qfm-all", ("l0",))] = fq
    fk = p.op("gpsimd", lambda e: e.nop(),
              reads=[("kfm", ("l0",), hp, ts2) for hp in range(8) for ts2 in range(8)],
              writes=[("F4",)])
    p.lastw[("kfm-all", ("l0",))] = fk

    emit_flash(ctx, qfm, kfm, vptiles, avfm, causal=True, key=("l0",))
    ctx.pop_scope()

    emit_epilogue_l0(ctx, xfm, xnfm, avfm, wo_in, w1qb, w2qtb, b1_in, b2_in,
                     x1fm_out, ("l0",))

    p.op("sync", lambda e: e.nop(),
         reads=[("x1fm", ("l0",), ds2, ts2) for ds2 in range(8) for ts2 in range(8)],
         writes=[("ENDL0",)])


_launches = {}


def _get(name, builder):
    if name not in _launches:
        _launches[name] = builder()
    return _launches[name]


def kernel(L0_attn_x, L0_attn_WQ, L0_attn_WK, L0_attn_WV, L0_attn_WO,
           L0_ffn_W1, L0_ffn_b1, L0_ffn_W2, L0_ffn_b2,
           L1_attn_W_q, L1_attn_W_k, L1_attn_W_v, L1_attn_W_o,
           L1_ffn_gate_w, L1_ffn_w1, L1_ffn_w2):
    cores = list(range(NCORES))
    x = np.ascontiguousarray(np.asarray(L0_attn_x, dtype=np.float32))
    wq = np.ascontiguousarray(np.asarray(L0_attn_WQ, np.float32).reshape(D, D))
    wk = np.ascontiguousarray(np.asarray(L0_attn_WK, np.float32).reshape(D, D))
    wv = np.ascontiguousarray(np.asarray(L0_attn_WV, np.float32).reshape(D, D))
    wo = np.ascontiguousarray(np.asarray(L0_attn_WO, np.float32).reshape(H * D, DH))
    w1 = np.ascontiguousarray(np.asarray(L0_ffn_W1, np.float32))
    b1 = np.ascontiguousarray(np.asarray(L0_ffn_b1, np.float32).reshape(32, 128))
    w2 = np.ascontiguousarray(np.asarray(L0_ffn_W2, np.float32))
    b2 = np.ascontiguousarray(np.asarray(L0_ffn_b2, np.float32).reshape(1, D))
    wq1 = np.ascontiguousarray(np.asarray(L1_attn_W_q, np.float32).reshape(H * D, DH))
    wk1 = np.ascontiguousarray(np.asarray(L1_attn_W_k, np.float32).reshape(H * D, DH))
    wv1 = np.ascontiguousarray(np.asarray(L1_attn_W_v, np.float32).reshape(H * D, DH))
    wo1 = np.ascontiguousarray(np.asarray(L1_attn_W_o, np.float32))
    gw = np.ascontiguousarray(np.asarray(L1_ffn_gate_w, np.float32))
    w1e = np.asarray(L1_ffn_w1, np.float32)
    w2e = np.asarray(L1_ffn_w2, np.float32)

    lAB = _get("AB", build_launch_AB)
    in_maps = [{"x": x[c], "wq": wq, "wk": wk, "wv": wv, "wo": wo,
                "w1": w1, "b1": b1, "w2": w2, "b2": b2,
                "wq1": wq1, "wk1": wk1, "wv1": wv1, "wo1": wo1, "gw": gw}
               for c in cores]
    resB = run_bass_kernel_spmd(lAB, in_maps, core_ids=cores)
    out1p = [np.ascontiguousarray(resB.results[c]["out1pfm"]) for c in cores]
    xn_all = np.ascontiguousarray(
        np.concatenate([resB.results[c]["xn1aug"] for c in cores], axis=0))
    route_all = np.ascontiguousarray(
        np.concatenate([resB.results[c]["route"] for c in cores], axis=0))
    counts_all = np.ascontiguousarray(
        np.concatenate([resB.results[c]["counts"] for c in cores], axis=0))

    lC = _get("C", build_launch_C)
    in_maps = []
    for c in cores:
        in_maps.append({
            "xn_all": xn_all, "route_all": route_all, "counts_all": counts_all,
            "w1e": np.ascontiguousarray(w1e[EPP * c:EPP * (c + 1)].reshape(EPP * D, FF)),
            "w2e": np.ascontiguousarray(w2e[EPP * c:EPP * (c + 1)].reshape(EPP * FF, D)),
            "mybase": np.full((1, 1), float(c * EPP * CAP), np.float32),
        })
    resC = run_bass_kernel_spmd(lC, in_maps, core_ids=cores)
    expert_all = np.ascontiguousarray(
        np.concatenate([resC.results[c]["expert_my"] for c in cores], axis=0))

    lD = _get("D", build_launch_D)
    in_maps = []
    for c in cores:
        ec = np.zeros((8, 1), np.float32)
        ec[c, 0] = 1.0
        in_maps.append({"expert_all": expert_all,
                        "route": np.ascontiguousarray(resB.results[c]["route"]),
                        "counts_all": counts_all, "out1pfm": out1p[c],
                        "ecore": ec})
    resD = run_bass_kernel_spmd(lD, in_maps, core_ids=cores)
    out = np.stack([resD.results[c]["out"] for c in cores])
    return out.astype(np.float32)
